# version 06
# baseline (speedup 1.0000x reference)
import math
from contextlib import ExitStack

import numpy as np
import ml_dtypes

import concourse.bass as bass
import concourse.mybir as mybir
from concourse.bass_utils import run_bass_kernel_spmd
from concourse.alu_op_type import AluOpType

try:
    import jax
    jax.config.update("jax_compilation_cache_dir", "/tmp/jaxcache_kernel")
    jax.config.update("jax_persistent_cache_min_compile_time_secs", 0.0)
    jax.config.update("jax_persistent_cache_min_entry_size_bytes", 0)
except Exception:
    pass

F32 = mybir.dt.float32
BF16 = mybir.dt.bfloat16
U32 = mybir.dt.uint32
AX = mybir.AxisListType.X
AF = mybir.ActivationFunctionType
OP = AluOpType
BF = ml_dtypes.bfloat16

D = 2048
H = 16
DK = 128
NE = 8
DI = 2048
C = 32
SUB = 8
MID = 15
KCONV = 4

DEF_CFG = dict(T=2048, dump=())


class Sem:

    def __init__(self, nc, E, name):
        self.h = E(nc.semaphore(name))
        self.n = 0
        self.name = name

    def inc(self, instr, k=1):
        instr.then_inc(self.h, k)
        self.n += k
        return self.n

    def wait(self, eng, v=None):
        if v is None:
            v = self.n
        assert v <= self.n, f"wait on future value of {self.name}: {v} > {self.n}"
        eng.wait_ge(self.h, v)


def ceil_div(a, b):
    return (a + b - 1) // b



def build_program(cfg):
    T = cfg["T"]
    HT = T // 2
    NCH = T // C
    NTT = T // 512 if T >= 512 else 1
    TT = min(512, T)
    dump = set(cfg.get("dump", ()))

    nc = bass.Bass("TRN2", target_bir_lowering=False)

    def din(name, shape, dt=F32):
        return nc.dram_tensor(name, shape, dt, kind="ExternalInput").ap()

    x_seq = din("x_seq", [T, D])
    xT_in = din("xT_in", [D, T])
    WT_in = {m: din(f"WT_{m}", [D, D]) for m in ("q", "k", "v", "a", "g", "o")}
    WbT_in = din("WbT_in", [D, H])
    WrT_in = din("WrT_in", [D, NE])
    convw_arr = {m: din(f"cwa_{m}", [128, H * KCONV]) for m in ("q", "k", "v")}
    convb_arr = {m: din(f"cba_{m}", [128, H]) for m in ("q", "k", "v")}
    ba_arr = din("ba_arr", [128, H])
    half_flag = din("half_flag", [1, 1], U32)
    Wmat = {m: din(f"W{m}", [D, D]) for m in ("q", "k", "v", "a", "g", "o")}
    ba = din("ba", [D])
    Wb = din("Wb", [H, D])
    bb = din("bb", [H])
    convw = {m: din(f"conv_{m}_w", [D, 1, KCONV]) for m in ("q", "k", "v")}
    convb = {m: din(f"conv_{m}_b", [D]) for m in ("q", "k", "v")}
    ln_g = din("ln_g", [DK])
    ln_b = din("ln_b", [DK])
    Wr = din("router_w", [NE, D])
    br = din("router_b", [NE])
    gup = din("gup", [NE, D, 2 * DI])
    gup_b = din("gup_b", [NE, 2 * DI])
    dwn = din("down", [NE, DI, D])
    dwn_b = din("down_b", [NE, D])
    iden = din("iden", [128, 128], BF16)
    maskpq = din("maskpq", [C, C])
    onescol = din("onescol", [128, 1], BF16)
    iota8 = din("iota8", [NE])

    y_out = nc.dram_tensor("y_out", [HT, D], F32, kind="ExternalOutput").ap()

    def scr(name, shape, dt):
        kind = "ExternalOutput" if name in dump else "Internal"
        return nc.dram_tensor(name, shape, dt, kind=kind).ap()

    s_xbf = scr("s_xbf", [D, T], BF16)
    s_wbf = {m: scr(f"s_wbf_{m}", [D, D], BF16) for m in ("q", "k", "v", "a", "g", "o")}
    s_wbf_b = scr("s_wbf_b", [D, H], BF16)
    s_wbf_r = scr("s_wbf_r", [D, NE], BF16)
    s_q = scr("s_q", [NCH, 128, H * C], BF16)
    s_k = scr("s_k", [NCH, 128, H * C], BF16)
    s_v = scr("s_v", [NCH, 128, H * C], BF16)
    s_g = scr("s_g", [NCH, 128, H * C], BF16)
    s_a = scr("s_a", [NCH, 128, H * C], F32)
    s_beta = scr("s_beta", [T // C, H * C], BF16)
    s_G = scr("s_G", [2, C, H * C], BF16)
    s_Gar = scr("s_Gar", [2, C // SUB, SUB * H * SUB], BF16)
    s_ogdt = scr("s_ogdt", [128, H * cfg["T"]], BF16)
    Ws = [C - SUB * (s + 1) for s in range(C // SUB - 1)]
    s_gbd = scr("s_gbd", [2, 128, H * sum(Ws)], BF16)
    gbd_off = np.cumsum([0] + [H * w for w in Ws]).tolist()
    s_stats = scr("s_stats", [2, 2 * H * C], F32)
    s_gupbf = scr("s_gupbf", [NE, D, 2 * DI], BF16)
    s_dwnbf = scr("s_dwnbf", [NE, DI, D], BF16)
    s_gubbf = scr("s_gubbf", [NE, 2 * DI], BF16)
    s_dwbbf = scr("s_dwbbf", [NE, D], BF16)
    s_x2bf = scr("s_x2bf", [HT, D], BF16)
    s_x2f = scr("s_x2f", [HT, D], F32)
    s_act = scr("s_act", [NE, HT, DI], BF16)
    s_og = scr("s_og", [1, 1], F32) if False else None

    with ExitStack() as ctx:
        E = ctx.enter_context

        s_p0x = Sem(nc, E, "s_p0x")
        s_p0w = {m: Sem(nc, E, f"s_p0w{m}") for m in ("q", "k", "v", "a", "g", "o")}
        s_p0s = Sem(nc, E, "s_p0s")
        s_con = Sem(nc, E, "s_con")
        s_xT = Sem(nc, E, "s_xT")
        s_wc = [Sem(nc, E, f"s_wc{i}") for i in range(2)]
        s_sp = [Sem(nc, E, f"s_sp{i}") for i in range(2)]
        s_spb = Sem(nc, E, "s_spb")
        s_te = Sem(nc, E, "s_te")
        s_dv = Sem(nc, E, "s_dv")
        s_ac = Sem(nc, E, "s_ac")
        s_out = Sem(nc, E, "s_out")

        g = nc.gpsimd
        np0 = 4
        for i in range(np0):
            r0, r1 = D * i // np0, D * (i + 1) // np0
            s_p0x.inc(g.dma_start(out=s_xbf[r0:r1, :], in_=xT_in[r0:r1, :]), 16)
        for m in ("q", "k", "v", "a", "g", "o"):
            for i in range(4):
                r0, r1 = D * i // 4, D * (i + 1) // 4
                s_p0w[m].inc(g.dma_start(out=s_wbf[m][r0:r1, :], in_=WT_in[m][r0:r1, :]), 16)
        s_p0s.inc(g.dma_start(out=s_wbf_b[:], in_=WbT_in[:]), 16)
        s_p0s.inc(g.dma_start(out=s_wbf_r[:], in_=WrT_in[:]), 16)
        s_p0f = Sem(nc, E, "s_p0f")
        for e_ in range(NE):
            for i_ in range(4):
                r0, r1 = D * i_ // 4, D * (i_ + 1) // 4
                s_p0f.inc(g.dma_start(out=s_gupbf[e_, r0:r1, :], in_=gup[e_, r0:r1, :]), 16)
        for e_ in range(NE):
            for i_ in range(2):
                r0, r1 = DI * i_ // 2, DI * (i_ + 1) // 2
                s_p0f.inc(g.dma_start(out=s_dwnbf[e_, r0:r1, :], in_=dwn[e_, r0:r1, :]), 16)
        s_p0f.inc(g.dma_start(out=s_gubbf[:], in_=gup_b[:]), 16)
        s_p0f.inc(g.dma_start(out=s_dwbbf[:], in_=dwn_b[:]), 16)

        idt = E(nc.sbuf_tensor("idt", [128, 128], BF16))
        ones_sb = E(nc.sbuf_tensor("ones_sb", [128, 1], BF16))
        lngt = E(nc.sbuf_tensor("lngt", [128, 1], F32))
        lnbt = E(nc.sbuf_tensor("lnbt", [128, 1], F32))
        flagt = E(nc.sbuf_tensor("flagt", [1, 1], U32))
        maskt = E(nc.sbuf_tensor("maskt", [C, C], F32))

        sy = nc.sync
        s_con.inc(sy.dma_start(out=idt[:], in_=iden[:]), 16)
        s_con.inc(sy.dma_start(out=ones_sb[:], in_=onescol[:]), 16)
        s_con.inc(sy.dma_start(out=lngt[:], in_=ln_g[:].unsqueeze(1)), 16)
        s_con.inc(sy.dma_start(out=lnbt[:], in_=ln_b[:].unsqueeze(1)), 16)
        s_con.inc(sy.dma_start(out=flagt[:], in_=half_flag[:]), 16)
        s_con.inc(sy.dma_start(out=maskt[:], in_=maskpq[:]), 16)

        with ExitStack() as p1ctx:
            E1 = p1ctx.enter_context
            xT = E1(nc.sbuf_tensor("xT", [128, 16 * T], BF16))
            betat = E1(nc.sbuf_tensor("betat", [H, T], F32))
            wcol = [E1(nc.sbuf_tensor(f"wcol{i}", [128, 16 * 512], BF16)) for i in range(2)]
            wbT = E1(nc.sbuf_tensor("wbT", [128, 16 * H], BF16))
            stag = [E1(nc.sbuf_tensor(f"stag{i}", [128, T + 3], F32)) for i in range(2)]
            cvout = [E1(nc.sbuf_tensor(f"cvout{i}", [128, T], F32)) for i in range(2)]
            obf = [E1(nc.sbuf_tensor(f"obf{i}", [128, T], BF16)) for i in range(2)]
            convwt = {m: E1(nc.sbuf_tensor(f"convw_{m}", [128, H * KCONV], F32))
                      for m in ("q", "k", "v")}
            convbt = {m: E1(nc.sbuf_tensor(f"convb_{m}", [128, H], F32))
                      for m in ("q", "k", "v")}
            bat = E1(nc.sbuf_tensor("bat", [128, H], F32))
            bbt = E1(nc.sbuf_tensor("bbt", [H, 1], F32))
            psum = E1(nc.psum_tensor("psum", [128, 4096], F32))

            for m in ("q", "k", "v"):
                s_con.inc(sy.dma_start(out=convwt[m][:], in_=convw_arr[m][:]), 16)
                s_con.inc(sy.dma_start(out=convbt[m][:], in_=convb_arr[m][:]), 16)
            s_con.inc(sy.dma_start(out=bat[:], in_=ba_arr[:]), 16)
            s_con.inc(sy.dma_start(out=bbt[:], in_=bb[:].unsqueeze(1)), 16)
            sy.wait_ge(s_p0s.h, s_p0s.n)
            s_con.inc(sy.dma_start(
                out=wbT[:].rearrange("p (dt h) -> p dt h", h=H),
                in_=s_wbf_b[:].rearrange("(dt p) h -> p dt h", p=128)), 16)
            sy.wait_ge(s_p0x.h, s_p0x.n)
            for i2 in range(2):
                s_xT.inc(sy.dma_start(
                    out=xT[:].rearrange("p (dt t) -> p dt t", t=T)[:, i2 * 8:(i2 + 1) * 8, :],
                    in_=s_xbf[:].rearrange("(dt p) t -> p dt t", p=128)
                        [:, i2 * 8:(i2 + 1) * 8, :]), 16)

            v_ = nc.vector
            a_ = nc.scalar
            t_ = nc.tensor

            t_.wait_ge(s_xT.h, s_xT.n)
            t_.wait_ge(s_con.h, s_con.n)
            for tt_i in range(ceil_div(T, 512)):
                t0 = tt_i * 512
                tl = min(512, T - t0)
                for dt_i in range(16):
                    mm = t_.matmul(out=psum[0:H, 2048 + t0:2048 + t0 + tl],
                                   lhsT=wbT[:, dt_i * H:(dt_i + 1) * H],
                                   rhs=xT[:, dt_i * T + t0:dt_i * T + t0 + tl],
                                   start=(dt_i == 0), stop=(dt_i == 15))
                s_te.inc(mm)
            te_beta = s_te.n
            a_.wait_ge(s_te.h, te_beta)
            a_.wait_ge(s_con.h, s_con.n)
            s_ac.inc(a_.activation(out=betat[:], in_=psum[0:H, 2048:2048 + T],
                                   func=AF.Sigmoid, bias=bbt[0:H, 0:1], scale=1.0))
            g.wait_ge(s_ac.h, s_ac.n)
            s_spb.inc(g.dma_start(
                out=s_beta[:].rearrange("c (h t) -> h c t", t=C),
                in_=betat[:].rearrange("p (c t) -> p c t", t=C)), 16)

            mats = [("q", s_q, "conv"), ("k", s_k, "conv"), ("v", s_v, "conv"),
                    ("a", s_a, "sig_f32"), ("g", s_g, "sig_bf16")]
            p1_ac = {}
            for mi, (m, s_dst, kind) in enumerate(mats):
                for f512 in range(4):
                    par = (mi * 4 + f512) % 2
                    wc = wcol[par]
                    swc = s_wc[par]
                    if mi * 4 + f512 >= 2:
                        sy.wait_ge(s_te.h, s_te.n)
                    sy.wait_ge(s_p0w[m].h, s_p0w[m].n)
                    swc.inc(sy.dma_start(
                        out=wc[:].rearrange("p (dt f) -> p dt f", f=512),
                        in_=s_wbf[m][:].rearrange("(dt p) f -> p dt f", p=128)
                            [:, :, f512 * 512:(f512 + 1) * 512]), 16)
                    for f128 in range(4):
                        ft = f512 * 4 + f128
                        fpar = ft % 2
                        st = stag[fpar]
                        co = cvout[fpar]
                        ob = obf[fpar]
                        ssp = s_sp[fpar]
                        a_.wait_ge(ssp.h, ssp.n)
                        v_.wait_ge(ssp.h, ssp.n)
                        for tt_i in range(ceil_div(T, TT)):
                            t0 = tt_i * TT
                            bank = (tt_i % 2)
                            pso = psum[:, bank * 512:bank * 512 + TT]
                            t_.wait_ge(swc.h, swc.n)
                            if bank in p1_ac:
                                t_.wait_ge(s_ac.h, p1_ac[bank])
                            for dt_i in range(16):
                                mm = t_.matmul(
                                    out=pso,
                                    lhsT=wc[:, dt_i * 512 + f128 * 128:
                                            dt_i * 512 + f128 * 128 + 128],
                                    rhs=xT[:, dt_i * T + t0:dt_i * T + t0 + TT],
                                    start=(dt_i == 0), stop=(dt_i == 15))
                            s_te.inc(mm)
                            a_.wait_ge(s_te.h, s_te.n)
                            if kind == "conv":
                                s_ac.inc(a_.copy(out=st[:, 3 + t0:3 + t0 + TT], in_=pso))
                            elif kind == "sig_f32":
                                s_ac.inc(a_.activation(
                                    out=co[:, t0:t0 + TT], in_=pso,
                                    func=AF.Sigmoid, bias=bat[:, ft:ft + 1], scale=1.0))
                            else:
                                s_ac.inc(a_.activation(
                                    out=ob[:, t0:t0 + TT], in_=pso,
                                    func=AF.Sigmoid))
                            p1_ac[bank] = s_ac.n
                        if kind == "conv":
                            v_.wait_ge(s_ac.h, s_ac.n)
                            v_.wait_ge(s_con.h, s_con.n)
                            v_.memset(st[:, 0:3], 0.0)
                            v_.drain()
                            v_.tensor_scalar(out=co[:, 0:T], in0=st[:, 0:T],
                                             scalar1=convwt[m][:, ft * KCONV:ft * KCONV + 1],
                                             scalar2=None, op0=OP.mult)
                            v_.drain()
                            for j in range(1, KCONV):
                                tap = v_.scalar_tensor_tensor(
                                    out=co[:, 0:T], in0=st[:, j:j + T],
                                    scalar=convwt[m][:, ft * KCONV + j:ft * KCONV + j + 1],
                                    in1=co[:, 0:T], op0=OP.mult, op1=OP.add)
                                if j < KCONV - 1:
                                    v_.drain()
                            s_dv.inc(tap)
                            a_.wait_ge(s_dv.h, s_dv.n)
                            if m == "k":
                                s_ac.inc(a_.activation(out=co[:, 0:T], in_=co[:, 0:T],
                                                       func=AF.Silu,
                                                       bias=convbt[m][:, ft:ft + 1],
                                                       scale=1.0))
                                v_.wait_ge(s_ac.h, s_ac.n)
                                s_dv.inc(v_.tensor_scalar(
                                    out=ob[:, 0:T], in0=co[:, 0:T],
                                    scalar1=float(DK) ** -0.5, scalar2=None, op0=OP.mult))
                                g.wait_ge(s_dv.h, s_dv.n)
                            else:
                                s_ac.inc(a_.activation(out=ob[:, 0:T], in_=co[:, 0:T],
                                                       func=AF.Silu,
                                                       bias=convbt[m][:, ft:ft + 1],
                                                       scale=1.0))
                                g.wait_ge(s_ac.h, s_ac.n)
                            ssp.inc(g.dma_start(
                                out=s_dst[:, :, ft * C:(ft + 1) * C].transpose([1, 0, 2]),
                                in_=ob[:, 0:T].rearrange("p (n c) -> p n c", c=C)), 16)
                        elif kind == "sig_f32":
                            g.wait_ge(s_ac.h, s_ac.n)
                            ssp.inc(g.dma_start(
                                out=s_dst[:, :, ft * C:(ft + 1) * C].transpose([1, 0, 2]),
                                in_=co[:, 0:T].rearrange("p (n c) -> p n c", c=C)), 16)
                        else:
                            g.wait_ge(s_ac.h, s_ac.n)
                            ssp.inc(g.dma_start(
                                out=s_dst[:, :, ft * C:(ft + 1) * C].transpose([1, 0, 2]),
                                in_=ob[:, 0:T].rearrange("p (n c) -> p n c", c=C)), 16)
            p1_spill_n = [s_sp[0].n, s_sp[1].n]

        if cfg.get("p1_only", False):
            zt = E(nc.sbuf_tensor("zt_fin0", [128, D], F32))
            nc.vector.wait_ge(s_te.h, s_te.n)
            nc.vector.wait_ge(s_ac.h, s_ac.n)
            nc.vector.wait_ge(s_sp[0].h, s_sp[0].n)
            nc.vector.wait_ge(s_sp[1].h, s_sp[1].n)
            s_dv.inc(nc.vector.memset(zt[:], 0.0))
            g.wait_ge(s_dv.h, s_dv.n)
            g.wait_ge(s_sp[0].h, p1_spill_n[0])
            g.wait_ge(s_sp[1].h, p1_spill_n[1])
            g.wait_ge(s_spb.h, s_spb.n)
            for r in range(ceil_div(HT, 128)):
                nr = min(128, HT - r * 128)
                s_out.inc(g.dma_start(out=y_out[r * 128:r * 128 + nr, :],
                                      in_=zt[0:nr, :]), 16)
            g.wait_ge(s_out.h, s_out.n)
            return nc


        v_ = nc.vector
        a_ = nc.scalar
        t_ = nc.tensor
        sy = nc.sync
        import bass_rust as _br

        s_in2 = [Sem(nc, E, f"s_in2{i}") for i in range(2)]
        s_gb2 = [Sem(nc, E, f"s_gb2{i}") for i in range(2)]
        s_gbr = [Sem(nc, E, f"s_gbr{i}") for i in range(2)]
        s_gbw = [Sem(nc, E, f"s_gbw{i}") for i in range(2)]
        s_gst = [Sem(nc, E, f"s_gst{i}") for i in range(2)]
        s_sst = [Sem(nc, E, f"s_sst{i}") for i in range(2)]
        s_stb = [Sem(nc, E, f"s_stb{i}") for i in range(2)]
        s_ogd = Sem(nc, E, "s_ogd")

        og_cm = nc.sbuf_tensor("og", [128, H * T], BF16)
        og = og_cm.__enter__()
        with ExitStack() as p2ctx:
            E2 = p2ctx.enter_context
            S_ = [E2(nc.sbuf_tensor(f"S{i}", [128, H * 128], F32)) for i in range(2)]
            tmpSc = E2(nc.sbuf_tensor("tmpSc", [128, H * 128], BF16))
            S0mT = E2(nc.sbuf_tensor("S0mT", [128, H * 128], BF16))
            kc = [E2(nc.sbuf_tensor(f"kc{i}", [128, H * C], BF16)) for i in range(2)]
            qc = [E2(nc.sbuf_tensor(f"qc{i}", [128, H * C], BF16)) for i in range(2)]
            vc = [E2(nc.sbuf_tensor(f"vc{i}", [128, H * C], BF16)) for i in range(2)]
            gc = [E2(nc.sbuf_tensor(f"gc{i}", [128, H * C], BF16)) for i in range(2)]
            ac2 = [E2(nc.sbuf_tensor(f"ac{i}", [128, H * C], F32)) for i in range(2)]
            bbc = [E2(nc.sbuf_tensor(f"bbc{i}", [128, H * C], BF16)) for i in range(2)]
            kpc = E2(nc.sbuf_tensor("kpc", [128, H * C], BF16))
            ra = E2(nc.sbuf_tensor("ra", [128, H * C], F32))
            Pt = E2(nc.sbuf_tensor("Pt", [128, H * (C + 1)], F32))
            rp = E2(nc.sbuf_tensor("rp", [128, H * (C + 1)], F32))
            rPm = E2(nc.sbuf_tensor("rPm", [128, H], F32))
            RC = E2(nc.sbuf_tensor("RC", [128, H], F32))
            Rt = E2(nc.sbuf_tensor("Rt", [128, H * C], F32))
            cv = E2(nc.sbuf_tensor("cv", [128, H * C], F32))
            U = E2(nc.sbuf_tensor("U", [128, H * C], F32))
            Y = E2(nc.sbuf_tensor("Y", [128, H * C], F32))
            Ybf = E2(nc.sbuf_tensor("Ybf", [128, H * C], BF16))
            Ybnd = E2(nc.sbuf_tensor("Ybnd", [128, 128], BF16))
            prod = E2(nc.sbuf_tensor("prod", [128, 128], F32))
            Gst = E2(nc.sbuf_tensor("Gst", [C, H * C], BF16))
            Pqm = E2(nc.sbuf_tensor("Pqm", [C, H * C], BF16))
            Gb = [E2(nc.sbuf_tensor(f"Gb{i}", [128, (C // SUB) * SUB * H * SUB], BF16))
                  for i in range(2)]
            Gbd = [E2(nc.sbuf_tensor(f"Gbd{i}", [128, H * sum(Ws)], BF16)) for i in range(2)]
            YTb = E2(nc.sbuf_tensor("YTb", [128, 128], BF16))
            ZTt = E2(nc.sbuf_tensor("ZTt", [C, H * 128], BF16))
            Ktm = E2(nc.sbuf_tensor("Ktm", [C, H * 128], BF16))
            rcb = [E2(nc.sbuf_tensor(f"rcb{i}", [128, 128], F32)) for i in range(2)]
            osc = E2(nc.sbuf_tensor("osc", [128, H * C], F32))
            obf2 = E2(nc.sbuf_tensor("obf2", [128, H * C], BF16))
            sqb = E2(nc.sbuf_tensor("sqb", [128, H * C], BF16))
            stbuf = E2(nc.sbuf_tensor("stbuf", [1, 2 * H * C], F32))
            stb = E2(nc.sbuf_tensor("stb", [128, 2 * H * C], F32))
            mu = E2(nc.sbuf_tensor("mu", [128, H * C], F32))
            varb = E2(nc.sbuf_tensor("varb", [128, H * C], F32))
            t1b = E2(nc.sbuf_tensor("t1b", [128, H * C], F32))
            onesf = E2(nc.sbuf_tensor("onesf", [128, 1], F32))
            epst = E2(nc.sbuf_tensor("epst", [128, 1], F32))
            zz = E2(nc.sbuf_tensor("zz", [128, C], F32))
            ps2 = E2(nc.psum_tensor("ps2", [128, 4096], F32))

            psB = ps2[:, 0:H * C]
            psG = ps2[0:C, 1024:1024 + H * C]
            psPq = ps2[0:C, 2048:2048 + H * C]
            psO = ps2[:, 1024:1024 + H * C]
            def psSt(tr):
                return ps2[:, 2048 + (tr % 2) * 512:2048 + (tr % 2) * 512 + 128]
            psTb = ps2[:, 3072:3584].bitcast(BF16)
            psStats = ps2[0:1, 3072:4096]

            for eng in (v_, a_, t_, sy, g):
                eng.wait_ge(s_te.h, s_te.n)
                eng.wait_ge(s_dv.h, s_dv.n)
                eng.wait_ge(s_ac.h, s_ac.n)
                eng.wait_ge(s_sp[0].h, s_sp[0].n)
                eng.wait_ge(s_sp[1].h, s_sp[1].n)
                eng.wait_ge(s_spb.h, s_spb.n)

            s_gz = Sem(nc, E, "s_gz")
            s_dv.inc(v_.memset(onesf[:], 1.0))
            v_.memset(epst[:], 1e-5)
            v_.memset(zz[:], 0.0)
            v_.memset(Pt[:], 1.0)
            v_.memset(S_[0][:], 0.0)
            ZW = min(H * T, H * sum(Ws))
            s_dv.inc(v_.memset(og[:, 0:ZW], 0.0))
            v_.drain()
            g.wait_ge(s_dv.h, s_dv.n)
            for par_ in range(2):
                for zo in range(0, H * sum(Ws), ZW):
                    zw = min(ZW, H * sum(Ws) - zo)
                    s_gz.inc(g.dma_start(out=s_gbd[par_][:, zo:zo + zw],
                                         in_=og[:, 0:zw]), 16)
            v_.wait_ge(s_gz.h, s_gz.n)

            te_end = {}
            dv_end = {}
            ac_end = {}
            dv_state = {-1: s_dv.n}

            HW0 = H * Ws[0]
            NSUB = C // SUB

            def gbd_write_ap(par, s):
                W = Ws[s]
                HWs = H * sum(Ws)
                off = int(par) * 128 * HWs + gbd_off[s]
                return _br.AP(tensor=s_gbd.tensor, offset=off,
                              ap=[[HWs, SUB], [SUB * HWs + W, H], [1, W]])

            def ap2d(tens, offset, stride, count):
                return _br.AP(tensor=tens.tensor, offset=offset,
                              ap=[[tens.ap[0][0], tens.ap[0][1]], [stride, count]])

            for c in range(NCH):
                par = c % 2
                Sprev = S_[c % 2]
                Snew = S_[(c + 1) % 2]

                if c >= 2:
                    sy.wait_ge(s_te.h, te_end[c - 2])
                    sy.wait_ge(s_dv.h, dv_end[c - 2])
                    sy.wait_ge(s_ac.h, ac_end[c - 2])
                if c == 0:
                    sy.wait_ge(s_sp[0].h, p1_spill_n[0])
                    sy.wait_ge(s_sp[1].h, p1_spill_n[1])
                    sy.wait_ge(s_spb.h, s_spb.n)
                s_in2[par].inc(sy.dma_start(out=kc[par][:], in_=s_k[c]), 16)
                s_in2[par].inc(sy.dma_start(out=qc[par][:], in_=s_q[c]), 16)
                s_in2[par].inc(sy.dma_start(out=vc[par][:], in_=s_v[c]), 16)
                if c >= 2:
                    a_.wait_ge(s_te.h, te_end[c - 2])
                    a_.wait_ge(s_dv.h, dv_end[c - 2])
                s_in2[par].inc(a_.dma_start(out=gc[par][:], in_=s_g[c]), 16)
                s_in2[par].inc(a_.dma_start(out=ac2[par][:], in_=s_a[c]), 16)
                s_in2[par].inc(a_.dma_start(
                    out=bbc[par][:],
                    in_=s_beta[c].unsqueeze(0).broadcast_to([128, H * C])), 16)
                in_ld = s_in2[par].n

                v_.wait_ge(s_in2[par].h, in_ld)
                v_.reciprocal(out=ra[:], in_=ac2[par][:])
                v_.drain()
                for tr in range(H):
                    v_.tensor_tensor_scan(
                        out=Pt[:, tr * (C + 1) + 1:(tr + 1) * (C + 1)],
                        data0=ac2[par][:, tr * C:(tr + 1) * C],
                        data1=zz[:], initial=onesf[:, 0:1],
                        op0=OP.mult, op1=OP.add)
                    v_.drain()
                v_.reciprocal(out=rp[:], in_=Pt[:])
                v_.drain()
                rp3 = rp[:].rearrange("p (h c) -> p h c", c=C + 1)
                Pt3 = Pt[:].rearrange("p (h c) -> p h c", c=C + 1)
                v_.tensor_copy(out=rPm[:].unsqueeze(2),
                               in_=rp3[:, :, MID + 1:MID + 2])
                v_.drain()
                v_.tensor_tensor(out=RC[:].unsqueeze(2), in0=Pt3[:, :, C:C + 1],
                                 in1=rPm[:].unsqueeze(2), op=OP.mult)
                v_.drain()
                for tr in range(H):
                    v_.scalar_tensor_tensor(
                        out=cv[:, tr * C:(tr + 1) * C],
                        in0=rp[:, tr * (C + 1):tr * (C + 1) + C],
                        scalar=Pt[:, tr * (C + 1) + MID + 1:tr * (C + 1) + MID + 2],
                        in1=vc[par][:, tr * C:(tr + 1) * C],
                        op0=OP.mult, op1=OP.mult)
                    v_.tensor_scalar(
                        out=Rt[:, tr * C:(tr + 1) * C],
                        in0=Pt[:, tr * (C + 1) + 1:(tr + 1) * (C + 1)],
                        scalar1=rPm[:, tr:tr + 1], scalar2=None, op0=OP.mult)
                    v_.drain()
                v_.wait_ge(s_in2[par].h, in_ld)
                v_.tensor_tensor(out=kpc[:], in0=kc[par][:], in1=bbc[par][:], op=OP.mult)
                v_.drain()
                v_.wait_ge(s_dv.h, dv_state[c - 1])
                for tr in range(H):
                    ts_op = v_.tensor_scalar(
                        out=tmpSc[:, tr * 128:(tr + 1) * 128],
                        in0=Sprev[:, tr * 128:(tr + 1) * 128],
                        scalar1=Pt[:, tr * (C + 1) + MID + 1:tr * (C + 1) + MID + 2],
                        scalar2=None, op0=OP.mult)
                    if tr < H - 1:
                        v_.drain()
                s_dv.inc(ts_op)
                dv_pro = s_dv.n

                t_.wait_ge(s_dv.h, dv_pro)
                for tr in range(H):
                    slot = tr % 8
                    t_.wait_ge(s_ac.h, s_ac.n)
                    s_te.inc(t_.transpose(out=psTb[:, slot * 128:(slot + 1) * 128],
                                          in_=tmpSc[:, tr * 128:(tr + 1) * 128],
                                          identity=idt[:]))
                    a_.wait_ge(s_te.h, s_te.n)
                    s_ac.inc(a_.copy(out=S0mT[:, tr * 128:(tr + 1) * 128],
                                     in_=psTb[:, slot * 128:(slot + 1) * 128]))
                ac_s0mt = s_ac.n

                for tr in range(H):
                    t_.wait_ge(s_dv.h, dv_pro)
                    mmg = t_.matmul(out=psG[:, tr * C:(tr + 1) * C],
                                    lhsT=kpc[:, tr * C:(tr + 1) * C],
                                    rhs=kc[par][:, tr * C:(tr + 1) * C],
                                    start=(tr == 0), stop=(tr == H - 1))
                    mmp = t_.matmul(out=psPq[:, tr * C:(tr + 1) * C],
                                    lhsT=kpc[:, tr * C:(tr + 1) * C],
                                    rhs=qc[par][:, tr * C:(tr + 1) * C],
                                    start=(tr == 0), stop=(tr == H - 1))
                s_te.inc(mmp)
                te_gram = s_te.n
                a_.wait_ge(s_te.h, te_gram)
                s_ac.inc(a_.copy(out=Gst[:], in_=psG[:]))
                ac_gs = s_ac.n
                v_.wait_ge(s_te.h, te_gram)
                mask3 = maskt[:].unsqueeze(1).broadcast_to([C, H, C])
                s_dv.inc(v_.tensor_tensor(
                    out=Pqm[:].rearrange("p (h c) -> p h c", c=C),
                    in0=psPq[:].rearrange("p (h c) -> p h c", c=C),
                    in1=mask3, op=OP.mult))
                dv_pqm = s_dv.n

                g.wait_ge(s_ac.h, ac_gs)
                if c >= 2:
                    g.wait_ge(s_te.h, te_end[c - 2])
                for s in range(NSUB):
                    s_gst[par].inc(g.dma_start(
                        out=s_Gar[par, s].rearrange("(j x) -> j x", x=H * SUB),
                        in_=Gst[SUB * s:SUB * s + SUB, :]
                            .rearrange("p (h c) -> p h c", c=C)[:, :, SUB * s:SUB * s + SUB]),
                        16)
                if c < 2:
                    g.wait_ge(s_gz.h, s_gz.n)
                for s in range(NSUB - 1):
                    s_gbw[par].inc(g.dma_start(
                        out=gbd_write_ap(par, s),
                        in_=Gst[SUB * s:SUB * s + SUB, :]
                            .rearrange("p (h c) -> p h c", c=C)[:, :, SUB * (s + 1):]),
                        16)
                gst_n = s_gst[par].n
                gbw_n = s_gbw[par].n

                t_.wait_ge(s_ac.h, ac_s0mt)
                t_.wait_ge(s_ac.h, ac_gs)
                t_.wait_ge(s_dv.h, dv_pqm)
                if c >= 1:
                    t_.wait_ge(s_dv.h, dv_end[c - 1])
                for tr in range(H):
                    mb_ = t_.matmul(out=psB[:, tr * C:(tr + 1) * C],
                                    lhsT=S0mT[:, tr * 128:(tr + 1) * 128],
                                    rhs=kc[par][:, tr * C:(tr + 1) * C],
                                    start=(tr == 0), stop=False,
                                    skip_group_check=True)
                    mo_ = t_.matmul(out=psO[:, tr * C:(tr + 1) * C],
                                    lhsT=S0mT[:, tr * 128:(tr + 1) * 128],
                                    rhs=qc[par][:, tr * C:(tr + 1) * C],
                                    start=(tr == 0), stop=False)
                s_te.inc(mo_)
                te_base = s_te.n

                sub_dv = {}
                roll_te = {-1: te_base}
                gbr_cnt = {}

                sy.wait_ge(s_gst[par].h, gst_n)
                if c >= 2:
                    sy.wait_ge(s_dv.h, dv_end[c - 2])
                s_gb2[par].inc(sy.dma_start(
                    out=Gb[par][:],
                    in_=s_Gar[par].rearrange("s x -> (s x)").unsqueeze(0)
                        .broadcast_to([128, NSUB * SUB * H * SUB])), 16)
                sy.wait_ge(s_gbw[par].h, gbw_n)
                if c >= 2:
                    sy.wait_ge(s_te.h, te_end[c - 2])
                s_gbr[par].inc(sy.dma_start(
                    out=Gbd[par][:], in_=s_gbd[par]), 16)
                gbr_all = s_gbr[par].n
                for s in range(NSUB):
                    sl = par
                    v_.wait_ge(s_te.h, roll_te[s - 1])
                    cv3 = cv[:].rearrange("p (h c) -> p h c", c=C)
                    U3 = U[:].rearrange("p (h c) -> p h c", c=C)
                    psB3 = psB.rearrange("p (h c) -> p h c", c=C)
                    v_.tensor_tensor(out=U3[:, :, SUB * s:SUB * (s + 1)],
                                     in0=cv3[:, :, SUB * s:SUB * (s + 1)],
                                     in1=psB3[:, :, SUB * s:SUB * (s + 1)],
                                     op=OP.subtract)
                    v_.drain()
                    v_.wait_ge(s_gb2[sl].h, s_gb2[sl].n)
                    ra3 = ra[:].rearrange("p (h c) -> p h c", c=C)
                    Y3 = Y[:].rearrange("p (h c) -> p h c", c=C)
                    Gb3 = Gb[sl][:, s * SUB * H * SUB:(s + 1) * SUB * H * SUB]\
                        .rearrange("p (j h t) -> p j h t", j=SUB, t=SUB)
                    for j in range(SUB):
                        t0 = SUB * s + j
                        last_op = v_.tensor_tensor(out=Y3[:, :, t0:t0 + 1],
                                                   in0=ra3[:, :, t0:t0 + 1],
                                                   in1=U3[:, :, t0:t0 + 1], op=OP.mult)
                        v_.drain()
                        if j < SUB - 1:
                            nrem = SUB - 1 - j
                            v_.tensor_tensor(
                                out=prod[:, 0:H * nrem].rearrange("p (h t) -> p h t", t=nrem),
                                in0=Y3[:, :, t0:t0 + 1].broadcast_to([128, H, nrem]),
                                in1=Gb3[:, j, :, j + 1:SUB], op=OP.mult)
                            v_.drain()
                            last_op = v_.tensor_tensor(
                                out=U3[:, :, t0 + 1:SUB * (s + 1)],
                                in0=U3[:, :, t0 + 1:SUB * (s + 1)],
                                in1=prod[:, 0:H * nrem].rearrange("p (h t) -> p h t", t=nrem),
                                op=OP.subtract)
                            v_.drain()
                    s_dv.inc(last_op)
                    sub_dv[s] = s_dv.n

                    if s < NSUB - 1:
                        s_dv.inc(v_.tensor_copy(
                            out=Ybnd[:].rearrange("p (h t) -> p h t", t=SUB),
                            in_=Y3[:, :, SUB * s:SUB * (s + 1)]))
                        v_.drain()
                        dv_ybnd = s_dv.n
                        slot = s % 8
                        t_.wait_ge(s_dv.h, dv_ybnd)
                        t_.wait_ge(s_ac.h, s_ac.n)
                        s_te.inc(t_.transpose(out=psTb[:, slot * 128:(slot + 1) * 128],
                                              in_=Ybnd[:], identity=idt[:]))
                        a_.wait_ge(s_te.h, s_te.n)
                        s_ac.inc(a_.copy(out=YTb[:], in_=psTb[:, slot * 128:(slot + 1) * 128]))
                        ac_ytb = s_ac.n
                        W = Ws[s]
                        t_.wait_ge(s_ac.h, ac_ytb)
                        t_.wait_ge(s_gbr[sl].h, gbr_all)
                        t_.wait_ge(s_dv.h, sub_dv[s])
                        for tr in range(H):
                            mr = t_.matmul(
                                out=psB[:, tr * C + SUB * (s + 1):(tr + 1) * C],
                                lhsT=YTb[:],
                                rhs=Gbd[sl][:, gbd_off[s] + tr * W:
                                            gbd_off[s] + (tr + 1) * W],
                                start=False, stop=(s == NSUB - 2 and tr == H - 1),
                                skip_group_check=True)
                        s_te.inc(mr)
                        roll_te[s] = s_te.n
                    else:
                        roll_te[s] = roll_te[s - 1]

                v_.wait_ge(s_dv.h, sub_dv[NSUB - 1])
                s_dv.inc(v_.tensor_copy(out=Ybf[:], in_=Y[:]))
                dv_ybf = s_dv.n
                for tr in range(H):
                    slot = tr % 4
                    t_.wait_ge(s_dv.h, dv_ybf)
                    t_.wait_ge(s_ac.h, s_ac.n)
                    s_te.inc(t_.transpose(out=psTb[0:C, slot * 128:(slot + 1) * 128],
                                          in_=Ybf[:, tr * C:(tr + 1) * C],
                                          identity=idt[:]))
                    a_.wait_ge(s_te.h, s_te.n)
                    s_ac.inc(a_.copy(out=ZTt[:, tr * 128:(tr + 1) * 128],
                                     in_=psTb[0:C, slot * 128:(slot + 1) * 128]))
                    slot2 = 4 + tr % 4
                    t_.wait_ge(s_ac.h, s_ac.n)
                    s_te.inc(t_.transpose(out=psTb[0:C, slot2 * 128:(slot2 + 1) * 128],
                                          in_=kpc[:, tr * C:(tr + 1) * C],
                                          identity=idt[:]))
                    a_.wait_ge(s_te.h, s_te.n)
                    s_ac.inc(a_.copy(out=Ktm[:, tr * 128:(tr + 1) * 128],
                                     in_=psTb[0:C, slot2 * 128:(slot2 + 1) * 128]))
                ac_zt = s_ac.n

                t_.wait_ge(s_ac.h, ac_zt)
                t_.wait_ge(s_dv.h, dv_pqm)
                for tr in range(H):
                    mo2 = t_.matmul(out=psO[:, tr * C:(tr + 1) * C],
                                    lhsT=ZTt[:, tr * 128:(tr + 1) * 128],
                                    rhs=Pqm[:, tr * C:(tr + 1) * C],
                                    start=False, stop=(tr == H - 1))
                s_te.inc(mo2)
                te_o = s_te.n

                st_dv_prev = {}
                for tr in range(H):
                    if tr >= 2:
                        t_.wait_ge(s_dv.h, st_dv_prev[tr - 2])
                    ms = t_.matmul(out=psSt(tr), lhsT=ZTt[:, tr * 128:(tr + 1) * 128],
                                   rhs=Ktm[:, tr * 128:(tr + 1) * 128],
                                   start=True, stop=True,
                                   skip_group_check=True)
                    s_te.inc(ms)
                    v_.wait_ge(s_te.h, s_te.n)
                    v_.tensor_scalar(out=rcb[tr % 2][:], in0=psSt(tr),
                                     scalar1=RC[:, tr:tr + 1], scalar2=None, op0=OP.mult)
                    v_.drain()
                    s_dv.inc(v_.scalar_tensor_tensor(
                        out=Snew[:, tr * 128:(tr + 1) * 128],
                        in0=Sprev[:, tr * 128:(tr + 1) * 128],
                        scalar=Pt[:, tr * (C + 1) + C:tr * (C + 1) + C + 1],
                        in1=rcb[tr % 2][:], op0=OP.mult, op1=OP.add))
                    st_dv_prev[tr] = s_dv.n
                dv_state[c] = s_dv.n

                v_.wait_ge(s_te.h, te_o)
                s_dv.inc(v_.tensor_tensor(out=osc[:], in0=psO[:], in1=Rt[:], op=OP.mult))
                dv_osc = s_dv.n
                v_.drain()
                s_dv.inc(v_.tensor_copy(out=obf2[:], in_=osc[:]))
                dv_obf = s_dv.n
                a_.wait_ge(s_dv.h, dv_osc)
                s_ac.inc(a_.activation(out=sqb[:], in_=osc[:], func=AF.Square))
                ac_sq = s_ac.n
                t_.wait_ge(s_dv.h, dv_obf)
                t_.wait_ge(s_ac.h, ac_sq)
                for rnd, rhs_t in ((0, obf2), (1, sqb)):
                    NHLF = H * C // 512
                    for ii in range(NHLF):
                        mst = t_.matmul(out=psStats[:, ii * 512:(ii + 1) * 512],
                                        lhsT=ones_sb[:],
                                        rhs=rhs_t[:, ii * 512:(ii + 1) * 512],
                                        start=True, stop=True)
                    s_te.inc(mst)
                    a_.wait_ge(s_te.h, s_te.n)
                    s_ac.inc(a_.copy(out=stbuf[0:1, rnd * H * C:(rnd + 1) * H * C],
                                     in_=psStats[:, 0:H * C]))
                    if rnd == 0:
                        t_.wait_ge(s_ac.h, s_ac.n)
                ac_st = s_ac.n
                g.wait_ge(s_ac.h, ac_st)
                if c >= 2:
                    g.wait_ge(s_dv.h, dv_end[c - 2])
                s_sst[par].inc(g.dma_start(out=s_stats[par, :].unsqueeze(0), in_=stbuf[:]), 16)
                g.wait_ge(s_sst[par].h, s_sst[par].n)
                s_stb[par].inc(g.dma_start(
                    out=stb[:], in_=s_stats[par, :].unsqueeze(0)
                        .broadcast_to([128, 2 * H * C])), 16)
                v_.wait_ge(s_stb[par].h, s_stb[par].n)
                s_dv.inc(v_.tensor_scalar(out=mu[:], in0=stb[:, 0:H * C],
                                          scalar1=1.0 / 128, scalar2=None, op0=OP.mult))
                a_.wait_ge(s_dv.h, s_dv.n)
                s_ac.inc(a_.activation(out=varb[:], in_=mu[:], func=AF.Square))
                v_.wait_ge(s_ac.h, s_ac.n)
                s_dv.inc(v_.scalar_tensor_tensor(out=varb[:], in0=stb[:, H * C:2 * H * C],
                                                 scalar=1.0 / 128, in1=varb[:],
                                                 op0=OP.mult, op1=OP.subtract))
                a_.wait_ge(s_dv.h, s_dv.n)
                s_ac.inc(a_.activation(out=varb[:], in_=varb[:], func=AF.Sqrt,
                                       bias=epst[:, 0:1]))
                v_.wait_ge(s_ac.h, s_ac.n)
                v_.reciprocal(out=varb[:], in_=varb[:])
                v_.drain()
                v_.tensor_tensor(out=t1b[:], in0=osc[:], in1=mu[:], op=OP.subtract)
                v_.drain()
                v_.tensor_tensor(out=t1b[:], in0=t1b[:], in1=varb[:], op=OP.mult)
                v_.drain()
                v_.scalar_tensor_tensor(out=t1b[:], in0=t1b[:], scalar=lngt[:, 0:1],
                                        in1=lnbt[:, 0:1].broadcast_to([128, H * C]),
                                        op0=OP.mult, op1=OP.add)
                v_.drain()
                og3 = og[:].rearrange("p (h t) -> p h t", t=T)
                s_dv.inc(v_.tensor_tensor(
                    out=og3[:, :, c * C:(c + 1) * C],
                    in0=t1b[:].rearrange("p (h c) -> p h c", c=C),
                    in1=gc[par][:].rearrange("p (h c) -> p h c", c=C), op=OP.mult))

                te_end[c] = s_te.n
                dv_end[c] = s_dv.n
                ac_end[c] = s_ac.n

            p2_te_end = s_te.n
            p2_dv_end = s_dv.n
            p2_ac_end = s_ac.n

            if "s_ogdt" in dump:
                g.wait_ge(s_dv.h, p2_dv_end)
                s_ogd.inc(g.dma_start(out=s_ogdt[:], in_=og[:]), 16)
                g.wait_ge(s_ogd.h, s_ogd.n)


        if cfg.get("p2_only", False):
            og_cm.__exit__(None, None, None)
            zt = E(nc.sbuf_tensor("zt_fin", [128, D], F32))
            nc.vector.wait_ge(s_te.h, s_te.n)
            nc.vector.wait_ge(s_ac.h, s_ac.n)
            nc.vector.wait_ge(s_ogd.h, s_ogd.n)
            nc.vector.drain()
            s_dv.inc(nc.vector.memset(zt[:], 0.0))
            g.wait_ge(s_dv.h, s_dv.n)
            for r in range(ceil_div(HT, 128)):
                nr = min(128, HT - r * 128)
                s_out.inc(g.dma_start(out=y_out[r * 128:r * 128 + nr, :],
                                      in_=zt[0:nr, :]), 16)
            g.wait_ge(s_out.h, s_out.n)
            return nc

        NTOK = HT // 128
        ps3 = E(nc.psum_tensor("ps3", [128, 4096], F32))
        with ExitStack() as p3ctx:
            E3 = p3ctx.enter_context
            woT = E3(nc.sbuf_tensor("woT", [128, H * D], BF16))
            xrt = [E3(nc.sbuf_tensor(f"xrt{i}", [128, D], F32)) for i in range(2)]
            t2f = E3(nc.sbuf_tensor("t2f", [128, D], F32))
            sqt = E3(nc.sbuf_tensor("sqt", [128, D], F32))
            x2ft = [E3(nc.sbuf_tensor(f"x2ft{i}", [128, D], F32)) for i in range(2)]
            x2bt = [E3(nc.sbuf_tensor(f"x2bt{i}", [128, D], BF16)) for i in range(2)]
            colt = E3(nc.sbuf_tensor("colt", [128, 8], F32))
            eps3 = E3(nc.sbuf_tensor("eps3", [128, 1], F32))
            s_woT = Sem(nc, E, "s_woT")
            s_xr = [Sem(nc, E, f"s_xr{i}") for i in range(2)]
            s_x2s = [Sem(nc, E, f"s_x2s{i}") for i in range(2)]

            for eng in (v_, a_, t_, sy, g):
                eng.wait_ge(s_te.h, p2_te_end)
                eng.wait_ge(s_dv.h, p2_dv_end)
                eng.wait_ge(s_ac.h, p2_ac_end)
            s_dv.inc(v_.memset(eps3[:], 1e-5))

            sy.wait_ge(s_p0w["o"].h, s_p0w["o"].n)
            for i2 in range(2):
                s_woT.inc(sy.dma_start(
                    out=woT[:].rearrange("p (dt f) -> p dt f", f=D)[:, i2 * 8:(i2 + 1) * 8, :],
                    in_=s_wbf["o"][:].rearrange("(dt p) f -> p dt f", p=128)
                        [:, i2 * 8:(i2 + 1) * 8, :]), 16)

            engs = [nc.tensor, nc.vector, nc.scalar, nc.sync, nc.gpsimd]
            for eng in engs:
                eng.wait_ge(s_con.h, s_con.n)
            rv = nc.values_load(flagt[0:1, 0:1],
                                engines=[mybir.EngineType.PE, mybir.EngineType.DVE,
                                         mybir.EngineType.Activation, mybir.EngineType.SP,
                                         mybir.EngineType.Pool])
            _sw_sems = [s_xr[0], s_xr[1], s_x2s[0], s_x2s[1], s_te, s_dv, s_ac]
            _sw_snap = [s.n for s in _sw_sems]
            for half in nc.Switch(engines=engs, index=[rv] * 5, n=2):
                if half == 1:
                    for _s, _v in zip(_sw_sems, _sw_snap):
                        _s.n = _v
                h0 = half * HT
                p3_dvj = {}
                for j in range(NTOK):
                    par3 = j % 2
                    tok = h0 + j * 128
                    if j >= 2:
                        sy.wait_ge(s_dv.h, s_dv.n)
                    s_xr[par3].inc(sy.dma_start(out=xrt[par3][:],
                                                in_=x_seq[tok:tok + 128, :]), 16)
                    t_.wait_ge(s_woT.h, s_woT.n)
                    pwo = (j % 2) * 2048
                    if j >= 2:
                        t_.wait_ge(s_dv.h, p3_dvj[j - 2])
                    for dcol in range(4):
                        for ft in range(H):
                            mw = t_.matmul(
                                out=ps3[:, pwo + dcol * 512:pwo + (dcol + 1) * 512],
                                lhsT=og[:, ft * T + h0 + j * 128:
                                        ft * T + h0 + (j + 1) * 128],
                                rhs=woT[:, ft * D + dcol * 512:ft * D + (dcol + 1) * 512],
                                start=(ft == 0), stop=(ft == H - 1))
                        s_te.inc(mw)
                    te_w = s_te.n
                    v_.wait_ge(s_te.h, te_w)
                    v_.wait_ge(s_xr[par3].h, s_xr[par3].n)
                    v_.tensor_tensor(out=t2f[:], in0=ps3[:, pwo:pwo + 2048],
                                     in1=xrt[par3][:], op=OP.add)
                    v_.drain()
                    s_dv.inc(v_.tensor_reduce(out=colt[:, 0:1].unsqueeze(2),
                                              in_=t2f[:].unsqueeze(1),
                                              axis=AX, op=OP.add))
                    a_.wait_ge(s_dv.h, s_dv.n)
                    s_ac.inc(a_.activation(out=sqt[:], in_=t2f[:], func=AF.Square,
                                           accum_out=colt[:, 1:2]))
                    v_.wait_ge(s_ac.h, s_ac.n)
                    v_.tensor_scalar(out=colt[:, 2:3], in0=colt[:, 0:1],
                                     scalar1=1.0 / D, scalar2=None, op0=OP.mult)
                    v_.drain()
                    v_.tensor_tensor(out=colt[:, 3:4], in0=colt[:, 2:3],
                                     in1=colt[:, 2:3], op=OP.mult)
                    v_.drain()
                    s_dv.inc(v_.scalar_tensor_tensor(out=colt[:, 4:5], in0=colt[:, 1:2],
                                                     scalar=1.0 / D, in1=colt[:, 3:4],
                                                     op0=OP.mult, op1=OP.subtract))
                    a_.wait_ge(s_dv.h, s_dv.n)
                    s_ac.inc(a_.activation(out=colt[:, 5:6], in_=colt[:, 4:5],
                                           func=AF.Sqrt, bias=eps3[:, 0:1]))
                    v_.wait_ge(s_ac.h, s_ac.n)
                    v_.reciprocal(out=colt[:, 6:7], in_=colt[:, 5:6])
                    v_.drain()
                    v_.tensor_scalar(out=x2ft[par3][:], in0=t2f[:],
                                     scalar1=colt[:, 2:3], scalar2=colt[:, 6:7],
                                     op0=OP.subtract, op1=OP.mult)
                    v_.drain()
                    s_dv.inc(v_.tensor_copy(out=x2bt[par3][:], in_=x2ft[par3][:]))
                    p3_dvj[j] = s_dv.n
                    g.wait_ge(s_dv.h, s_dv.n)
                    s_x2s[par3].inc(g.dma_start(out=s_x2f[j * 128:(j + 1) * 128, :],
                                                in_=x2ft[par3][:]), 16)
                    s_x2s[par3].inc(g.dma_start(out=s_x2bf[j * 128:(j + 1) * 128, :],
                                                in_=x2bt[par3][:]), 16)
            x2s_n = [s_x2s[0].n, s_x2s[1].n]
            p3_te = s_te.n
            p3_dv = s_dv.n
            p3_ac = s_ac.n

        og_cm.__exit__(None, None, None)
        with ExitStack() as p4ctx:
            E4 = p4ctx.enter_context
            wrT = E4(nc.sbuf_tensor("wrT", [128, H * NE], BF16))
            brb = E4(nc.sbuf_tensor("brb", [128, NE], F32))
            iot = E4(nc.sbuf_tensor("iot", [128, NE], F32))
            lgt = E4(nc.sbuf_tensor("lgt", [128, NE], F32))
            mxt = E4(nc.sbuf_tensor("mxt", [128, NE], F32))
            ixt = E4(nc.sbuf_tensor("ixt", [128, NE], U32))
            ixf = E4(nc.sbuf_tensor("ixf", [128, 2], F32))
            p12 = E4(nc.sbuf_tensor("p12", [128, 4], F32))
            scall = E4(nc.sbuf_tensor("scall", [128, NE * NTOK], F32))
            scp = E4(nc.sbuf_tensor("scp", [128, 32], F32))
            scpb = E4(nc.sbuf_tensor("scpb", [128, 32], BF16))
            scTb = E4(nc.sbuf_tensor("scTb", [32, HT], BF16))
            gubt = E4(nc.sbuf_tensor("gubt", [1, NE * 2 * DI], BF16))
            onesrow = E4(nc.sbuf_tensor("onesrow", [1, 128], BF16))
            dwbt = E4(nc.sbuf_tensor("dwbt", [NE, D], BF16))

            s_x2T = Sem(nc, E, "s_x2T")
            s_wrT2 = Sem(nc, E, "s_wrT2")
            s_gup2 = [Sem(nc, E, f"s_gup2{i}") for i in range(2)]
            s_acts = [Sem(nc, E, f"s_acts{i}") for i in range(2)]
            s_brb = Sem(nc, E, "s_brb")

            for eng in (v_, a_, t_, sy, g):
                eng.wait_ge(s_te.h, p3_te)
                eng.wait_ge(s_dv.h, p3_dv)
                eng.wait_ge(s_ac.h, p3_ac)
            guctx = ExitStack()
            E4g = guctx.enter_context
            x2T = E4g(nc.sbuf_tensor("x2T", [128, H * HT], BF16))
            sy.wait_ge(s_x2s[0].h, x2s_n[0])
            sy.wait_ge(s_x2s[1].h, x2s_n[1])
            TG = min(512, HT)
            for dt_i in range(16):
                for tg in range(HT // TG):
                    s_x2T.inc(sy.dma_start_transpose(
                        out=x2T[:, dt_i * HT + tg * TG:dt_i * HT + (tg + 1) * TG],
                        in_=s_x2bf[tg * TG:(tg + 1) * TG,
                                   dt_i * 128:(dt_i + 1) * 128]), 16)
            sy.wait_ge(s_p0s.h, s_p0s.n)
            s_wrT2.inc(sy.dma_start(
                out=wrT[:].rearrange("p (dt e) -> p dt e", e=NE),
                in_=s_wbf_r[:].rearrange("(dt p) e -> p dt e", p=128)), 16)
            s_brb.inc(sy.dma_start(out=brb[:], in_=br[:].unsqueeze(0)
                                   .broadcast_to([128, NE])), 16)
            s_brb.inc(sy.dma_start(out=iot[:], in_=iota8[:].unsqueeze(0)
                                   .broadcast_to([128, NE])), 16)
            sy.wait_ge(s_p0f.h, s_p0f.n)
            s_brb.inc(sy.dma_start(out=gubt[:],
                                   in_=s_gubbf[:].rearrange("e j -> (e j)").unsqueeze(0)), 16)
            s_dv.inc(v_.memset(onesrow[:], 1.0))
            s_brb.inc(sy.dma_start(out=dwbt[:], in_=s_dwbbf[:]), 16)

            v_.memset(scp[:], 0.0)
            v_.drain()
            for j in range(NTOK):
                t_.wait_ge(s_x2T.h, s_x2T.n)
                t_.wait_ge(s_wrT2.h, s_wrT2.n)
                if j >= 1:
                    t_.wait_ge(s_dv.h, s_dv.n)
                    t_.wait_ge(s_ac.h, s_ac.n)
                for dt_i in range(16):
                    mr4 = t_.matmul(out=ps3[:, 3584:3584 + NE],
                                    lhsT=x2T[:, dt_i * HT + j * 128:dt_i * HT + (j + 1) * 128],
                                    rhs=wrT[:, dt_i * NE:(dt_i + 1) * NE],
                                    start=(dt_i == 0), stop=(dt_i == 15))
                s_te.inc(mr4)
                v_.wait_ge(s_te.h, s_te.n)
                v_.wait_ge(s_brb.h, s_brb.n)
                v_.tensor_tensor(out=lgt[:], in0=ps3[:, 3584:3584 + NE], in1=brb[:],
                                 op=OP.add)
                v_.drain()
                v_.max(out=mxt[:], in_=lgt[:])
                v_.drain()
                v_.max_index(out=ixt[:], in_max=mxt[:], in_values=lgt[:])
                v_.drain()
                v_.tensor_copy(out=ixf[:], in_=ixt[:, 0:2])
                v_.drain()
                v_.tensor_tensor(out=p12[:, 2:3], in0=mxt[:, 0:1], in1=mxt[:, 1:2],
                                 op=OP.subtract)
                v_.drain()
                s_dv.inc(v_.tensor_copy(out=p12[:, 3:4], in_=p12[:, 2:3]))
                a_.wait_ge(s_dv.h, s_dv.n)
                s_ac.inc(a_.activation(out=p12[:, 0:1], in_=p12[:, 3:4], func=AF.Sigmoid))
                v_.wait_ge(s_ac.h, s_ac.n)
                v_.tensor_scalar(out=p12[:, 1:2], in0=p12[:, 0:1], scalar1=-1.0,
                                 scalar2=1.0, op0=OP.mult, op1=OP.add)
                v_.drain()
                sc_j = scall[:, j * NE:(j + 1) * NE]
                v_.tensor_scalar(out=sc_j, in0=iot[:], scalar1=ixf[:, 0:1],
                                 scalar2=None, op0=OP.is_equal)
                v_.drain()
                v_.tensor_scalar(out=sc_j, in0=sc_j, scalar1=p12[:, 0:1],
                                 scalar2=None, op0=OP.mult)
                v_.drain()
                v_.tensor_scalar(out=scp[:, 0:NE], in0=iot[:], scalar1=ixf[:, 1:2],
                                 scalar2=None, op0=OP.is_equal)
                v_.drain()
                v_.scalar_tensor_tensor(out=sc_j, in0=scp[:, 0:NE],
                                        scalar=p12[:, 1:2], in1=sc_j,
                                        op0=OP.mult, op1=OP.add)
                v_.drain()
                v_.memset(scp[:, 0:32], 0.0)
                v_.drain()
                v_.tensor_copy(out=scp[:, 0:NE], in_=sc_j)
                v_.drain()
                s_dv.inc(v_.tensor_copy(out=scpb[:], in_=scp[:, 0:32]))
                psT4s = ps3[:, 3968:4032].bitcast(BF16)
                t_.wait_ge(s_dv.h, s_dv.n)
                s_te.inc(t_.transpose(out=psT4s[0:32, 0:128], in_=scpb[:],
                                      identity=idt[:]))
                a_.wait_ge(s_te.h, s_te.n)
                s_ac.inc(a_.copy(out=scTb[0:32, j * 128:(j + 1) * 128],
                                 in_=psT4s[0:32, 0:128]))
            dv_scores = s_dv.n

            glu_dv = {}
            gupb = [E4g(nc.sbuf_tensor(f"gupb{i}", [128, 16 * 1024], BF16)) for i in range(2)]
            actrow = [E4g(nc.sbuf_tensor(f"actrow{i}", [128, DI], BF16)) for i in range(NTOK)]
            gmn = E4g(nc.sbuf_tensor("gmn", [128, 256], F32))
            upc = E4g(nc.sbuf_tensor("upc", [128, 256], F32))
            sgm = E4g(nc.sbuf_tensor("sgm", [128, 256], F32))
            actf = E4g(nc.sbuf_tensor("actf", [128, 256], F32))
            for e in range(NE):
                for jh in range(4):
                    sl4 = (e * 4 + jh) % 2
                    if e * 4 + jh >= 2:
                        a_.wait_ge(s_te.h, s_te.n)
                    s_gup2[sl4].inc(a_.dma_start(
                        out=gupb[sl4][:].rearrange("p (dt j) -> p dt j", j=1024),
                        in_=s_gupbf[e].rearrange("(dt p) j -> p dt j", p=128)
                            [:, :, jh * 1024:(jh + 1) * 1024]), 16)
                    for j512 in range(2):
                        jcol = jh * 1024 + j512 * 512
                        for j in range(NTOK):
                            bank = j % 8
                            t_.wait_ge(s_gup2[sl4].h, s_gup2[sl4].n)
                            t_.wait_ge(s_x2T.h, s_x2T.n)
                            if bank in glu_dv:
                                t_.wait_ge(s_dv.h, glu_dv[bank])
                            for dt_i in range(16):
                                mg = t_.matmul(
                                    out=ps3[:, bank * 512:(bank + 1) * 512],
                                    lhsT=x2T[:, dt_i * HT + j * 128:
                                             dt_i * HT + (j + 1) * 128],
                                    rhs=gupb[sl4][:, dt_i * 1024 + j512 * 512:
                                                  dt_i * 1024 + j512 * 512 + 512],
                                    start=(dt_i == 0), stop=False)
                            t_.wait_ge(s_dv.h, s_dv.n)
                            mg = t_.matmul(
                                out=ps3[:, bank * 512:(bank + 1) * 512],
                                lhsT=onesrow[:],
                                rhs=gubt[0:1, e * 2 * DI + jcol:e * 2 * DI + jcol + 512],
                                start=False, stop=True)
                            s_te.inc(mg)
                            v_.wait_ge(s_te.h, s_te.n)
                            pg = ps3[:, bank * 512:(bank + 1) * 512]
                            pg3 = pg.rearrange("p (a two) -> p a two", two=2)
                            v_.tensor_scalar(out=gmn[:], in0=pg3[:, :, 0:1].squeeze(2),
                                             scalar1=7.0, scalar2=None, op0=OP.min)
                            v_.drain()
                            s_dv.inc(v_.tensor_scalar(
                                out=upc[:], in0=pg3[:, :, 1:2].squeeze(2),
                                scalar1=7.0, scalar2=-7.0, op0=OP.min, op1=OP.max))
                            a_.wait_ge(s_dv.h, s_dv.n)
                            s_ac.inc(a_.activation(out=sgm[:], in_=gmn[:],
                                                   func=AF.Sigmoid, scale=1.702))
                            v_.wait_ge(s_ac.h, s_ac.n)
                            v_.tensor_tensor(out=sgm[:], in0=gmn[:], in1=sgm[:],
                                             op=OP.mult)
                            v_.drain()
                            v_.scalar_tensor_tensor(out=actf[:], in0=upc[:], scalar=1.0,
                                                    in1=sgm[:], op0=OP.add, op1=OP.mult)
                            v_.drain()
                            arow = actrow[j]
                            v_.wait_ge(s_acts[j % 2].h, s_acts[j % 2].n)
                            s_dv.inc(v_.tensor_scalar(
                                out=arow[:, jh * 512 + j512 * 256:
                                         jh * 512 + (j512 + 1) * 256],
                                in0=actf[:], scalar1=scall[:, j * NE + e:j * NE + e + 1],
                                scalar2=None, op0=OP.mult))
                            glu_dv[bank] = s_dv.n
                            v_.drain()
                            if jh == 3 and j512 == 1:
                                g.wait_ge(s_dv.h, s_dv.n)
                                s_acts[j % 2].inc(g.dma_start(
                                    out=s_act[e, j * 128:(j + 1) * 128, :],
                                    in_=arow[:]), 16)
            acts_n = [s_acts[0].n, s_acts[1].n]
            guctx.close()
            for eng in (v_, a_, t_, sy, g):
                eng.wait_ge(s_te.h, s_te.n)
                eng.wait_ge(s_dv.h, s_dv.n)
                eng.wait_ge(s_ac.h, s_ac.n)
                eng.wait_ge(s_acts[0].h, acts_n[0])
                eng.wait_ge(s_acts[1].h, acts_n[1])

            ffnout = E4(nc.sbuf_tensor("ffnout", [128, NTOK * 512 * 4 // 4], F32)) \
                if False else None
            p4f = E4(nc.sbuf_tensor("p4f", [128, NTOK * 2048], F32))
            dwnb = [E4(nc.sbuf_tensor(f"dwnb{i}", [128, 512], BF16)) for i in range(2)]
            actT4 = [E4(nc.sbuf_tensor(f"actT4{i}", [128, HT], BF16)) for i in range(2)]
            s_dwn2 = [Sem(nc, E, f"s_dwn2{i}") for i in range(2)]
            s_actT = [Sem(nc, E, f"s_actT{i}") for i in range(2)]

            for hcol in range(4):
                t_.wait_ge(s_dv.h, dv_scores)
                if hcol > 0:
                    t_.wait_ge(s_dv.h, s_dv.n)
                t_.wait_ge(s_brb.h, s_brb.n)
                for j in range(NTOK):
                    mb4 = t_.matmul(out=ps3[:, j * 512:(j + 1) * 512],
                                    lhsT=scTb[0:NE, j * 128:(j + 1) * 128],
                                    rhs=dwbt[0:NE, hcol * 512:(hcol + 1) * 512],
                                    start=True, stop=False, skip_group_check=True)
                s_te.inc(mb4)
                for e in range(NE):
                    for dt_i in range(16):
                        slw = (e * 16 + dt_i) % 2
                        g.wait_ge(s_te.h, s_te.n)
                        s_dwn2[slw].inc(g.dma_start(
                            out=dwnb[slw][:],
                            in_=s_dwnbf[e, dt_i * 128:(dt_i + 1) * 128,
                                        hcol * 512:(hcol + 1) * 512]), 16)
                        sy.wait_ge(s_acts[0].h, acts_n[0])
                        sy.wait_ge(s_acts[1].h, acts_n[1])
                        sy.wait_ge(s_te.h, s_te.n)
                        s_actT[slw].inc(sy.dma_start_transpose(
                            out=actT4[slw][:],
                            in_=s_act[e, :, dt_i * 128:(dt_i + 1) * 128]), 16)
                        t_.wait_ge(s_dwn2[slw].h, s_dwn2[slw].n)
                        t_.wait_ge(s_actT[slw].h, s_actT[slw].n)
                        last = (e == NE - 1 and dt_i == 15)
                        for j in range(NTOK):
                            md4 = t_.matmul(
                                out=ps3[:, j * 512:(j + 1) * 512],
                                lhsT=actT4[slw][:, j * 128:(j + 1) * 128],
                                rhs=dwnb[slw][:],
                                start=False, stop=(last and j == NTOK - 1),
                                skip_group_check=True)
                        s_te.inc(md4)
                v_.wait_ge(s_te.h, s_te.n)
                for j in range(NTOK):
                    ev = v_.tensor_copy(
                        out=p4f[:, j * 2048 + hcol * 512:j * 2048 + (hcol + 1) * 512],
                        in_=ps3[:, j * 512:(j + 1) * 512])
                s_dv.inc(ev)
                v_.drain()

            x2r4 = [E4(nc.sbuf_tensor(f"x2r4{i}", [128, D], F32)) for i in range(2)]
            yt4 = [E4(nc.sbuf_tensor(f"yt4{i}", [128, D], F32)) for i in range(2)]
            colt4 = E4(nc.sbuf_tensor("colt4", [128, 8], F32))
            sqt4 = E4(nc.sbuf_tensor("sqt4", [128, D], F32))
            eps4 = E4(nc.sbuf_tensor("eps4", [128, 1], F32))
            s_xr4 = [Sem(nc, E, f"s_xr4{i}") for i in range(2)]
            s_dv.inc(v_.memset(eps4[:], 1e-5))
            for j in range(NTOK):
                par4 = j % 2
                if j >= 2:
                    sy.wait_ge(s_dv.h, s_dv.n)
                sy.wait_ge(s_x2s[0].h, x2s_n[0])
                sy.wait_ge(s_x2s[1].h, x2s_n[1])
                s_xr4[par4].inc(sy.dma_start(out=x2r4[par4][:],
                                             in_=s_x2f[j * 128:(j + 1) * 128, :]), 16)
                v_.wait_ge(s_xr4[par4].h, s_xr4[par4].n)
                v_.tensor_tensor(out=yt4[par4][:], in0=p4f[:, j * 2048:(j + 1) * 2048],
                                 in1=x2r4[par4][:], op=OP.add)
                v_.drain()
                s_dv.inc(v_.tensor_reduce(out=colt4[:, 0:1].unsqueeze(2),
                                          in_=yt4[par4][:].unsqueeze(1),
                                          axis=AX, op=OP.add))
                a_.wait_ge(s_dv.h, s_dv.n)
                s_ac.inc(a_.activation(out=sqt4[:], in_=yt4[par4][:], func=AF.Square,
                                       accum_out=colt4[:, 1:2]))
                v_.wait_ge(s_ac.h, s_ac.n)
                v_.tensor_scalar(out=colt4[:, 2:3], in0=colt4[:, 0:1],
                                 scalar1=1.0 / D, scalar2=None, op0=OP.mult)
                v_.drain()
                v_.tensor_tensor(out=colt4[:, 3:4], in0=colt4[:, 2:3],
                                 in1=colt4[:, 2:3], op=OP.mult)
                v_.drain()
                s_dv.inc(v_.scalar_tensor_tensor(out=colt4[:, 4:5], in0=colt4[:, 1:2],
                                                 scalar=1.0 / D, in1=colt4[:, 3:4],
                                                 op0=OP.mult, op1=OP.subtract))
                a_.wait_ge(s_dv.h, s_dv.n)
                s_ac.inc(a_.activation(out=colt4[:, 5:6], in_=colt4[:, 4:5],
                                       func=AF.Sqrt, bias=eps4[:, 0:1]))
                v_.wait_ge(s_ac.h, s_ac.n)
                v_.reciprocal(out=colt4[:, 6:7], in_=colt4[:, 5:6])
                v_.drain()
                s_dv.inc(v_.tensor_scalar(out=yt4[par4][:], in0=yt4[par4][:],
                                          scalar1=colt4[:, 2:3], scalar2=colt4[:, 6:7],
                                          op0=OP.subtract, op1=OP.mult))
                sy.wait_ge(s_dv.h, s_dv.n)
                s_out.inc(sy.dma_start(out=y_out[j * 128:(j + 1) * 128, :],
                                       in_=yt4[par4][:]), 16)
            sy.wait_ge(s_out.h, s_out.n)

    return nc


    return nc



def _consts():
    return {
        "iden": np.eye(128, dtype=BF),
        "maskpq": np.triu(np.ones((C, C), np.float32)),
        "onescol": np.ones((128, 1), BF),
        "iota8": np.arange(8, dtype=np.float32),
    }


def _derived(p):
    d = {}
    for m in ("q", "k", "v", "a", "g", "o"):
        d[f"WT_{m}"] = np.ascontiguousarray(p["attn_W" + m].T)
    d["WbT_in"] = np.ascontiguousarray(p["attn_Wb"].T)
    d["WrT_in"] = np.ascontiguousarray(p["ffn_router_weight"].T)
    for m in ("q", "k", "v"):
        d[f"cwa_{m}"] = np.ascontiguousarray(
            p[f"attn_conv_{m}_w"].reshape(16, 128, 4).transpose(1, 0, 2).reshape(128, 64))
        d[f"cba_{m}"] = np.ascontiguousarray(
            p[f"attn_conv_{m}_b"].reshape(16, 128).T)
    d["ba_arr"] = np.ascontiguousarray(p["attn_ba"].reshape(16, 128).T)
    return d


def kernel(**inputs):
    cfg = dict(DEF_CFG)
    nc = build_program(cfg)
    p = {k[3:] if k.startswith("L0_") else k: v for k, v in inputs.items()}
    x = np.ascontiguousarray(p["attn_x"])
    B = x.shape[0]
    consts = _consts()
    shared = dict(
        Wq=p["attn_Wq"], Wk=p["attn_Wk"], Wv=p["attn_Wv"], Wa=p["attn_Wa"],
        Wg=p["attn_Wg"], Wo=p["attn_Wo"], ba=p["attn_ba"], Wb=p["attn_Wb"],
        bb=p["attn_bb"],
        conv_q_w=p["attn_conv_q_w"], conv_q_b=p["attn_conv_q_b"],
        conv_k_w=p["attn_conv_k_w"], conv_k_b=p["attn_conv_k_b"],
        conv_v_w=p["attn_conv_v_w"], conv_v_b=p["attn_conv_v_b"],
        ln_g=p["attn_ln_g"], ln_b=p["attn_ln_b"],
        router_w=p["ffn_router_weight"], router_b=p["ffn_router_bias"],
        gup=p["ffn_gate_up_proj"], gup_b=p["ffn_gate_up_proj_bias"],
        down=p["ffn_down_proj"], down_b=p["ffn_down_proj_bias"],
        **consts,
    )
    shared.update(_derived(p))
    in_maps = []
    xTs = [np.ascontiguousarray(x[b].T) for b in range(B)]
    for c in range(8):
        b, h = c // 2, c % 2
        m = dict(shared)
        m["x_seq"] = x[b]
        m["xT_in"] = xTs[b]
        m["half_flag"] = np.array([[h]], dtype=np.uint32)
        in_maps.append(m)
    res = run_bass_kernel_spmd(nc, in_maps, core_ids=list(range(8)))
    out = np.empty((B, 2048, 2048), np.float32)
    for c in range(8):
        b, h = c // 2, c % 2
        out[b, h * 1024:(h + 1) * 1024, :] = res.results[c]["y_out"]
    return out


# version 07
# speedup vs baseline: 1.1431x; 1.1431x over previous
import math
from contextlib import ExitStack

import numpy as np
import ml_dtypes

import concourse.bass as bass
import concourse.mybir as mybir
from concourse.bass_utils import run_bass_kernel_spmd
from concourse.alu_op_type import AluOpType

try:
    import jax
    jax.config.update("jax_compilation_cache_dir", "/tmp/jaxcache_kernel")
    jax.config.update("jax_persistent_cache_min_compile_time_secs", 0.0)
    jax.config.update("jax_persistent_cache_min_entry_size_bytes", 0)
except Exception:
    pass

F32 = mybir.dt.float32
BF16 = mybir.dt.bfloat16
U32 = mybir.dt.uint32
AX = mybir.AxisListType.X
AF = mybir.ActivationFunctionType
OP = AluOpType
BF = ml_dtypes.bfloat16

D = 2048
H = 16
DK = 128
NE = 8
DI = 2048
C = 32
SUB = 8
MID = 15
KCONV = 4

DEF_CFG = dict(T=2048, dump=())


class Sem:

    def __init__(self, nc, E, name):
        self.h = E(nc.semaphore(name))
        self.n = 0
        self.name = name

    def inc(self, instr, k=1):
        instr.then_inc(self.h, k)
        self.n += k
        return self.n

    def wait(self, eng, v=None):
        if v is None:
            v = self.n
        assert v <= self.n, f"wait on future value of {self.name}: {v} > {self.n}"
        eng.wait_ge(self.h, v)


def ceil_div(a, b):
    return (a + b - 1) // b



def build_program(cfg):
    T = cfg["T"]
    HT = T // 2
    NCH = T // C
    NTT = T // 512 if T >= 512 else 1
    TT = min(512, T)
    dump = set(cfg.get("dump", ()))

    nc = bass.Bass("TRN2", target_bir_lowering=False)

    def din(name, shape, dt=F32):
        return nc.dram_tensor(name, shape, dt, kind="ExternalInput").ap()

    x_seq = din("x_seq", [T, D])
    xT_in = din("xT_in", [D, T])
    WT_in = {m: din(f"WT_{m}", [D, D]) for m in ("q", "k", "v", "a", "g", "o")}
    WbT_in = din("WbT_in", [D, H])
    WrT_in = din("WrT_in", [D, NE])
    convw_arr = {m: din(f"cwa_{m}", [128, H * KCONV]) for m in ("q", "k", "v")}
    convb_arr = {m: din(f"cba_{m}", [128, H]) for m in ("q", "k", "v")}
    ba_arr = din("ba_arr", [128, H])
    half_flag = din("half_flag", [1, 1], U32)
    Wmat = {m: din(f"W{m}", [D, D]) for m in ("q", "k", "v", "a", "g", "o")}
    ba = din("ba", [D])
    Wb = din("Wb", [H, D])
    bb = din("bb", [H])
    convw = {m: din(f"conv_{m}_w", [D, 1, KCONV]) for m in ("q", "k", "v")}
    convb = {m: din(f"conv_{m}_b", [D]) for m in ("q", "k", "v")}
    ln_g = din("ln_g", [DK])
    ln_b = din("ln_b", [DK])
    Wr = din("router_w", [NE, D])
    br = din("router_b", [NE])
    gup = din("gup", [NE, D, 2 * DI])
    gup_b = din("gup_b", [NE, 2 * DI])
    dwn = din("down", [NE, DI, D])
    dwn_b = din("down_b", [NE, D])
    iden = din("iden", [128, 128], BF16)
    maskpq = din("maskpq", [C, C])
    onescol = din("onescol", [128, 1], BF16)
    iota8 = din("iota8", [NE])

    y_out = nc.dram_tensor("y_out", [HT, D], F32, kind="ExternalOutput").ap()

    def scr(name, shape, dt):
        kind = "ExternalOutput" if name in dump else "Internal"
        return nc.dram_tensor(name, shape, dt, kind=kind).ap()

    s_xbf = scr("s_xbf", [D, T], BF16)
    s_wbf = {m: scr(f"s_wbf_{m}", [D, D], BF16) for m in ("q", "k", "v", "a", "g", "o")}
    s_wbf_b = scr("s_wbf_b", [D, H], BF16)
    s_wbf_r = scr("s_wbf_r", [D, NE], BF16)
    s_q = scr("s_q", [NCH, 128, H * C], BF16)
    s_k = scr("s_k", [NCH, 128, H * C], BF16)
    s_v = scr("s_v", [NCH, 128, H * C], BF16)
    s_g = scr("s_g", [NCH, 128, H * C], BF16)
    s_a = scr("s_a", [NCH, 128, H * C], F32)
    s_beta = scr("s_beta", [T // C, H * C], BF16)
    s_G = scr("s_G", [2, C, H * C], BF16)
    s_Gar = scr("s_Gar", [2, C // SUB, SUB * H * SUB], BF16)
    s_ogdt = scr("s_ogdt", [128, H * cfg["T"]], BF16)
    Ws = [C - SUB * (s + 1) for s in range(C // SUB - 1)]
    s_gbd = scr("s_gbd", [2, 128, H * sum(Ws)], BF16)
    gbd_off = np.cumsum([0] + [H * w for w in Ws]).tolist()
    s_stats = scr("s_stats", [2, 2 * H * C], F32)
    s_gupbf = scr("s_gupbf", [NE, D, 2 * DI], BF16)
    s_dwnbf = scr("s_dwnbf", [NE, DI, D], BF16)
    s_gubbf = scr("s_gubbf", [NE, 2 * DI], BF16)
    s_dwbbf = scr("s_dwbbf", [NE, D], BF16)
    s_x2bf = scr("s_x2bf", [HT, D], BF16)
    s_x2f = scr("s_x2f", [HT, D], F32)
    s_act = scr("s_act", [NE, HT, DI], BF16)
    s_og = scr("s_og", [1, 1], F32) if False else None

    with ExitStack() as ctx:
        E = ctx.enter_context

        s_p0x = Sem(nc, E, "s_p0x")
        s_p0w = {m: Sem(nc, E, f"s_p0w{m}") for m in ("q", "k", "v", "a", "g", "o")}
        s_p0s = Sem(nc, E, "s_p0s")
        s_con = Sem(nc, E, "s_con")
        s_xT = Sem(nc, E, "s_xT")
        s_wc = [Sem(nc, E, f"s_wc{i}") for i in range(2)]
        s_sp = [Sem(nc, E, f"s_sp{i}") for i in range(2)]
        s_spb = Sem(nc, E, "s_spb")
        s_te = Sem(nc, E, "s_te")
        s_dv = Sem(nc, E, "s_dv")
        s_ac = Sem(nc, E, "s_ac")
        s_out = Sem(nc, E, "s_out")

        g = nc.gpsimd
        np0 = 4
        for i in range(np0):
            r0, r1 = D * i // np0, D * (i + 1) // np0
            s_p0x.inc(g.dma_start(out=s_xbf[r0:r1, :], in_=xT_in[r0:r1, :]), 16)
        for m in ("q", "k", "v", "a", "g", "o"):
            for i in range(4):
                r0, r1 = D * i // 4, D * (i + 1) // 4
                s_p0w[m].inc(g.dma_start(out=s_wbf[m][r0:r1, :], in_=WT_in[m][r0:r1, :]), 16)
        s_p0s.inc(g.dma_start(out=s_wbf_b[:], in_=WbT_in[:]), 16)
        s_p0s.inc(g.dma_start(out=s_wbf_r[:], in_=WrT_in[:]), 16)
        s_p0f = Sem(nc, E, "s_p0f")
        for e_ in range(NE):
            for i_ in range(4):
                r0, r1 = D * i_ // 4, D * (i_ + 1) // 4
                s_p0f.inc(g.dma_start(out=s_gupbf[e_, r0:r1, :], in_=gup[e_, r0:r1, :]), 16)
        for e_ in range(NE):
            for i_ in range(2):
                r0, r1 = DI * i_ // 2, DI * (i_ + 1) // 2
                s_p0f.inc(g.dma_start(out=s_dwnbf[e_, r0:r1, :], in_=dwn[e_, r0:r1, :]), 16)
        s_p0f.inc(g.dma_start(out=s_gubbf[:], in_=gup_b[:]), 16)
        s_p0f.inc(g.dma_start(out=s_dwbbf[:], in_=dwn_b[:]), 16)

        idt = E(nc.sbuf_tensor("idt", [128, 128], BF16))
        ones_sb = E(nc.sbuf_tensor("ones_sb", [128, 1], BF16))
        lngt = E(nc.sbuf_tensor("lngt", [128, 1], F32))
        lnbt = E(nc.sbuf_tensor("lnbt", [128, 1], F32))
        flagt = E(nc.sbuf_tensor("flagt", [1, 1], U32))
        maskt = E(nc.sbuf_tensor("maskt", [C, C], F32))

        sy = nc.sync
        s_con.inc(sy.dma_start(out=idt[:], in_=iden[:]), 16)
        s_con.inc(sy.dma_start(out=ones_sb[:], in_=onescol[:]), 16)
        s_con.inc(sy.dma_start(out=lngt[:], in_=ln_g[:].unsqueeze(1)), 16)
        s_con.inc(sy.dma_start(out=lnbt[:], in_=ln_b[:].unsqueeze(1)), 16)
        s_con.inc(sy.dma_start(out=flagt[:], in_=half_flag[:]), 16)
        s_con.inc(sy.dma_start(out=maskt[:], in_=maskpq[:]), 16)

        with ExitStack() as p1ctx:
            E1 = p1ctx.enter_context
            xT = E1(nc.sbuf_tensor("xT", [128, 16 * T], BF16))
            betat = E1(nc.sbuf_tensor("betat", [H, T], F32))
            wcol = [E1(nc.sbuf_tensor(f"wcol{i}", [128, 16 * 512], BF16)) for i in range(2)]
            wbT = E1(nc.sbuf_tensor("wbT", [128, 16 * H], BF16))
            stag = [E1(nc.sbuf_tensor(f"stag{i}", [128, T + 3], F32)) for i in range(2)]
            cvout = [E1(nc.sbuf_tensor(f"cvout{i}", [128, T], F32)) for i in range(2)]
            obf = [E1(nc.sbuf_tensor(f"obf{i}", [128, T], BF16)) for i in range(2)]
            convwt = {m: E1(nc.sbuf_tensor(f"convw_{m}", [128, H * KCONV], F32))
                      for m in ("q", "k", "v")}
            convbt = {m: E1(nc.sbuf_tensor(f"convb_{m}", [128, H], F32))
                      for m in ("q", "k", "v")}
            bat = E1(nc.sbuf_tensor("bat", [128, H], F32))
            bbt = E1(nc.sbuf_tensor("bbt", [H, 1], F32))
            psum = E1(nc.psum_tensor("psum", [128, 4096], F32))

            for m in ("q", "k", "v"):
                s_con.inc(sy.dma_start(out=convwt[m][:], in_=convw_arr[m][:]), 16)
                s_con.inc(sy.dma_start(out=convbt[m][:], in_=convb_arr[m][:]), 16)
            s_con.inc(sy.dma_start(out=bat[:], in_=ba_arr[:]), 16)
            s_con.inc(sy.dma_start(out=bbt[:], in_=bb[:].unsqueeze(1)), 16)
            sy.wait_ge(s_p0s.h, s_p0s.n)
            s_con.inc(sy.dma_start(
                out=wbT[:].rearrange("p (dt h) -> p dt h", h=H),
                in_=s_wbf_b[:].rearrange("(dt p) h -> p dt h", p=128)), 16)
            sy.wait_ge(s_p0x.h, s_p0x.n)
            for i2 in range(2):
                s_xT.inc(sy.dma_start(
                    out=xT[:].rearrange("p (dt t) -> p dt t", t=T)[:, i2 * 8:(i2 + 1) * 8, :],
                    in_=s_xbf[:].rearrange("(dt p) t -> p dt t", p=128)
                        [:, i2 * 8:(i2 + 1) * 8, :]), 16)

            v_ = nc.vector
            a_ = nc.scalar
            t_ = nc.tensor

            t_.wait_ge(s_xT.h, s_xT.n)
            t_.wait_ge(s_con.h, s_con.n)
            for tt_i in range(ceil_div(T, 512)):
                t0 = tt_i * 512
                tl = min(512, T - t0)
                for dt_i in range(16):
                    mm = t_.matmul(out=psum[0:H, 2048 + t0:2048 + t0 + tl],
                                   lhsT=wbT[:, dt_i * H:(dt_i + 1) * H],
                                   rhs=xT[:, dt_i * T + t0:dt_i * T + t0 + tl],
                                   start=(dt_i == 0), stop=(dt_i == 15))
                s_te.inc(mm)
            te_beta = s_te.n
            a_.wait_ge(s_te.h, te_beta)
            a_.wait_ge(s_con.h, s_con.n)
            s_ac.inc(a_.activation(out=betat[:], in_=psum[0:H, 2048:2048 + T],
                                   func=AF.Sigmoid, bias=bbt[0:H, 0:1], scale=1.0))
            g.wait_ge(s_ac.h, s_ac.n)
            s_spb.inc(g.dma_start(
                out=s_beta[:].rearrange("c (h t) -> h c t", t=C),
                in_=betat[:].rearrange("p (c t) -> p c t", t=C)), 16)

            mats = [("q", s_q, "conv"), ("k", s_k, "conv"), ("v", s_v, "conv"),
                    ("a", s_a, "sig_f32"), ("g", s_g, "sig_bf16")]
            p1_ac = {}
            for mi, (m, s_dst, kind) in enumerate(mats):
                for f512 in range(4):
                    par = (mi * 4 + f512) % 2
                    wc = wcol[par]
                    swc = s_wc[par]
                    if mi * 4 + f512 >= 2:
                        sy.wait_ge(s_te.h, s_te.n)
                    sy.wait_ge(s_p0w[m].h, s_p0w[m].n)
                    swc.inc(sy.dma_start(
                        out=wc[:].rearrange("p (dt f) -> p dt f", f=512),
                        in_=s_wbf[m][:].rearrange("(dt p) f -> p dt f", p=128)
                            [:, :, f512 * 512:(f512 + 1) * 512]), 16)
                    for f128 in range(4):
                        ft = f512 * 4 + f128
                        fpar = ft % 2
                        st = stag[fpar]
                        co = cvout[fpar]
                        ob = obf[fpar]
                        ssp = s_sp[fpar]
                        a_.wait_ge(ssp.h, ssp.n)
                        v_.wait_ge(ssp.h, ssp.n)
                        for tt_i in range(ceil_div(T, TT)):
                            t0 = tt_i * TT
                            bank = (tt_i % 2)
                            pso = psum[:, bank * 512:bank * 512 + TT]
                            t_.wait_ge(swc.h, swc.n)
                            if bank in p1_ac:
                                t_.wait_ge(s_ac.h, p1_ac[bank])
                            for dt_i in range(16):
                                mm = t_.matmul(
                                    out=pso,
                                    lhsT=wc[:, dt_i * 512 + f128 * 128:
                                            dt_i * 512 + f128 * 128 + 128],
                                    rhs=xT[:, dt_i * T + t0:dt_i * T + t0 + TT],
                                    start=(dt_i == 0), stop=(dt_i == 15))
                            s_te.inc(mm)
                            a_.wait_ge(s_te.h, s_te.n)
                            if kind == "conv":
                                s_ac.inc(a_.copy(out=st[:, 3 + t0:3 + t0 + TT], in_=pso))
                            elif kind == "sig_f32":
                                s_ac.inc(a_.activation(
                                    out=co[:, t0:t0 + TT], in_=pso,
                                    func=AF.Sigmoid, bias=bat[:, ft:ft + 1], scale=1.0))
                            else:
                                s_ac.inc(a_.activation(
                                    out=ob[:, t0:t0 + TT], in_=pso,
                                    func=AF.Sigmoid))
                            p1_ac[bank] = s_ac.n
                        if kind == "conv":
                            v_.wait_ge(s_ac.h, s_ac.n)
                            v_.wait_ge(s_con.h, s_con.n)
                            v_.memset(st[:, 0:3], 0.0)
                            v_.drain()
                            v_.tensor_scalar(out=co[:, 0:T], in0=st[:, 0:T],
                                             scalar1=convwt[m][:, ft * KCONV:ft * KCONV + 1],
                                             scalar2=None, op0=OP.mult)
                            v_.drain()
                            for j in range(1, KCONV):
                                tap = v_.scalar_tensor_tensor(
                                    out=co[:, 0:T], in0=st[:, j:j + T],
                                    scalar=convwt[m][:, ft * KCONV + j:ft * KCONV + j + 1],
                                    in1=co[:, 0:T], op0=OP.mult, op1=OP.add)
                                if j < KCONV - 1:
                                    v_.drain()
                            s_dv.inc(tap)
                            a_.wait_ge(s_dv.h, s_dv.n)
                            if m == "k":
                                s_ac.inc(a_.activation(out=co[:, 0:T], in_=co[:, 0:T],
                                                       func=AF.Silu,
                                                       bias=convbt[m][:, ft:ft + 1],
                                                       scale=1.0))
                                v_.wait_ge(s_ac.h, s_ac.n)
                                s_dv.inc(v_.tensor_scalar(
                                    out=ob[:, 0:T], in0=co[:, 0:T],
                                    scalar1=float(DK) ** -0.5, scalar2=None, op0=OP.mult))
                                g.wait_ge(s_dv.h, s_dv.n)
                            else:
                                s_ac.inc(a_.activation(out=ob[:, 0:T], in_=co[:, 0:T],
                                                       func=AF.Silu,
                                                       bias=convbt[m][:, ft:ft + 1],
                                                       scale=1.0))
                                g.wait_ge(s_ac.h, s_ac.n)
                            ssp.inc(g.dma_start(
                                out=s_dst[:, :, ft * C:(ft + 1) * C].transpose([1, 0, 2]),
                                in_=ob[:, 0:T].rearrange("p (n c) -> p n c", c=C)), 16)
                        elif kind == "sig_f32":
                            g.wait_ge(s_ac.h, s_ac.n)
                            ssp.inc(g.dma_start(
                                out=s_dst[:, :, ft * C:(ft + 1) * C].transpose([1, 0, 2]),
                                in_=co[:, 0:T].rearrange("p (n c) -> p n c", c=C)), 16)
                        else:
                            g.wait_ge(s_ac.h, s_ac.n)
                            ssp.inc(g.dma_start(
                                out=s_dst[:, :, ft * C:(ft + 1) * C].transpose([1, 0, 2]),
                                in_=ob[:, 0:T].rearrange("p (n c) -> p n c", c=C)), 16)
            p1_spill_n = [s_sp[0].n, s_sp[1].n]

        if cfg.get("p1_only", False):
            zt = E(nc.sbuf_tensor("zt_fin0", [128, D], F32))
            nc.vector.wait_ge(s_te.h, s_te.n)
            nc.vector.wait_ge(s_ac.h, s_ac.n)
            nc.vector.wait_ge(s_sp[0].h, s_sp[0].n)
            nc.vector.wait_ge(s_sp[1].h, s_sp[1].n)
            s_dv.inc(nc.vector.memset(zt[:], 0.0))
            g.wait_ge(s_dv.h, s_dv.n)
            g.wait_ge(s_sp[0].h, p1_spill_n[0])
            g.wait_ge(s_sp[1].h, p1_spill_n[1])
            g.wait_ge(s_spb.h, s_spb.n)
            for r in range(ceil_div(HT, 128)):
                nr = min(128, HT - r * 128)
                s_out.inc(g.dma_start(out=y_out[r * 128:r * 128 + nr, :],
                                      in_=zt[0:nr, :]), 16)
            g.wait_ge(s_out.h, s_out.n)
            return nc


        v_ = nc.vector
        a_ = nc.scalar
        t_ = nc.tensor
        sy = nc.sync
        import bass_rust as _br

        s_in2 = [Sem(nc, E, f"s_in2{i}") for i in range(2)]
        s_gb2 = [Sem(nc, E, f"s_gb2{i}") for i in range(2)]
        s_gbr = [Sem(nc, E, f"s_gbr{i}") for i in range(2)]
        s_gbw = [Sem(nc, E, f"s_gbw{i}") for i in range(2)]
        s_gst = [Sem(nc, E, f"s_gst{i}") for i in range(2)]
        s_sst = [Sem(nc, E, f"s_sst{i}") for i in range(2)]
        s_stb = [Sem(nc, E, f"s_stb{i}") for i in range(2)]
        s_ogd = Sem(nc, E, "s_ogd")

        og_cm = nc.sbuf_tensor("og", [128, H * T], BF16)
        og = og_cm.__enter__()
        with ExitStack() as p2ctx:
            E2 = p2ctx.enter_context
            S_ = [E2(nc.sbuf_tensor(f"S{i}", [128, H * 128], F32)) for i in range(2)]
            tmpSc = E2(nc.sbuf_tensor("tmpSc", [128, H * 128], BF16))
            S0mT = E2(nc.sbuf_tensor("S0mT", [128, H * 128], BF16))
            kc = [E2(nc.sbuf_tensor(f"kc{i}", [128, H * C], BF16)) for i in range(2)]
            qc = [E2(nc.sbuf_tensor(f"qc{i}", [128, H * C], BF16)) for i in range(2)]
            vc = [E2(nc.sbuf_tensor(f"vc{i}", [128, H * C], BF16)) for i in range(2)]
            gc = [E2(nc.sbuf_tensor(f"gc{i}", [128, H * C], BF16)) for i in range(2)]
            ac2 = [E2(nc.sbuf_tensor(f"ac{i}", [128, H * C], F32)) for i in range(2)]
            bbc = [E2(nc.sbuf_tensor(f"bbc{i}", [128, H * C], BF16)) for i in range(2)]
            kpc = E2(nc.sbuf_tensor("kpc", [128, H * C], BF16))
            ra = E2(nc.sbuf_tensor("ra", [128, H * C], F32))
            Pt = E2(nc.sbuf_tensor("Pt", [128, H * (C + 1)], F32))
            rp = E2(nc.sbuf_tensor("rp", [128, H * (C + 1)], F32))
            rPm = E2(nc.sbuf_tensor("rPm", [128, H], F32))
            RC = E2(nc.sbuf_tensor("RC", [128, H], F32))
            Rt = E2(nc.sbuf_tensor("Rt", [128, H * C], F32))
            cv = E2(nc.sbuf_tensor("cv", [128, H * C], F32))
            U = E2(nc.sbuf_tensor("U", [128, H * C], F32))
            Y = E2(nc.sbuf_tensor("Y", [128, H * C], F32))
            Ybf = E2(nc.sbuf_tensor("Ybf", [128, H * C], BF16))
            Ybnd = E2(nc.sbuf_tensor("Ybnd", [128, 128], BF16))
            prod = E2(nc.sbuf_tensor("prod", [128, 128], F32))
            Gst = E2(nc.sbuf_tensor("Gst", [C, H * C], BF16))
            Pqm = E2(nc.sbuf_tensor("Pqm", [C, H * C], BF16))
            Gb = [E2(nc.sbuf_tensor(f"Gb{i}", [128, (C // SUB) * SUB * H * SUB], BF16))
                  for i in range(2)]
            Gbd = [E2(nc.sbuf_tensor(f"Gbd{i}", [128, H * sum(Ws)], BF16)) for i in range(2)]
            YTb = E2(nc.sbuf_tensor("YTb", [128, 128], BF16))
            ZTt = E2(nc.sbuf_tensor("ZTt", [C, H * 128], BF16))
            Ktm = E2(nc.sbuf_tensor("Ktm", [C, H * 128], BF16))
            rcb = [E2(nc.sbuf_tensor(f"rcb{i}", [128, 128], F32)) for i in range(2)]
            osc = E2(nc.sbuf_tensor("osc", [128, H * C], F32))
            obf2 = E2(nc.sbuf_tensor("obf2", [128, H * C], BF16))
            sqb = E2(nc.sbuf_tensor("sqb", [128, H * C], BF16))
            stbuf = E2(nc.sbuf_tensor("stbuf", [1, 2 * H * C], F32))
            stb = E2(nc.sbuf_tensor("stb", [128, 2 * H * C], F32))
            mu = E2(nc.sbuf_tensor("mu", [128, H * C], F32))
            varb = E2(nc.sbuf_tensor("varb", [128, H * C], F32))
            t1b = E2(nc.sbuf_tensor("t1b", [128, H * C], F32))
            onesf = E2(nc.sbuf_tensor("onesf", [128, 1], F32))
            epst = E2(nc.sbuf_tensor("epst", [128, 1], F32))
            zz = E2(nc.sbuf_tensor("zz", [128, C], F32))
            ps2 = E2(nc.psum_tensor("ps2", [128, 4096], F32))

            psB = ps2[:, 0:H * C]
            psG = ps2[0:C, 1024:1024 + H * C]
            psPq = ps2[0:C, 2048:2048 + H * C]
            psO = ps2[:, 1024:1024 + H * C]
            def psSt(tr):
                return ps2[:, 2048 + (tr % 2) * 512:2048 + (tr % 2) * 512 + 128]
            psTb = ps2[:, 3072:3584].bitcast(BF16)
            psStats = ps2[0:1, 3072:4096]

            for eng in (v_, a_, t_, sy, g):
                eng.wait_ge(s_te.h, s_te.n)
                eng.wait_ge(s_dv.h, s_dv.n)
                eng.wait_ge(s_ac.h, s_ac.n)
                eng.wait_ge(s_sp[0].h, s_sp[0].n)
                eng.wait_ge(s_sp[1].h, s_sp[1].n)
                eng.wait_ge(s_spb.h, s_spb.n)

            s_gz = Sem(nc, E, "s_gz")
            s_dv.inc(v_.memset(onesf[:], 1.0))
            v_.memset(epst[:], 1e-5)
            v_.memset(zz[:], 0.0)
            v_.memset(Pt[:], 1.0)
            v_.memset(S_[0][:], 0.0)
            ZW = min(H * T, H * sum(Ws))
            s_dv.inc(v_.memset(og[:, 0:ZW], 0.0))
            v_.drain()
            g.wait_ge(s_dv.h, s_dv.n)
            for par_ in range(2):
                for zo in range(0, H * sum(Ws), ZW):
                    zw = min(ZW, H * sum(Ws) - zo)
                    s_gz.inc(g.dma_start(out=s_gbd[par_][:, zo:zo + zw],
                                         in_=og[:, 0:zw]), 16)
            v_.wait_ge(s_gz.h, s_gz.n)

            te_end = {}
            dv_end = {}
            ac_end = {}
            dv_state = {-1: s_dv.n}

            HW0 = H * Ws[0]
            NSUB = C // SUB

            def gbd_write_ap(par, s):
                W = Ws[s]
                HWs = H * sum(Ws)
                off = int(par) * 128 * HWs + gbd_off[s]
                return _br.AP(tensor=s_gbd.tensor, offset=off,
                              ap=[[HWs, SUB], [SUB * HWs + W, H], [1, W]])

            def ap2d(tens, offset, stride, count):
                return _br.AP(tensor=tens.tensor, offset=offset,
                              ap=[[tens.ap[0][0], tens.ap[0][1]], [stride, count]])

            for c in range(NCH):
                par = c % 2
                Sprev = S_[c % 2]
                Snew = S_[(c + 1) % 2]

                if c >= 2:
                    sy.wait_ge(s_te.h, te_end[c - 2])
                    sy.wait_ge(s_dv.h, dv_end[c - 2])
                    sy.wait_ge(s_ac.h, ac_end[c - 2])
                if c == 0:
                    sy.wait_ge(s_sp[0].h, p1_spill_n[0])
                    sy.wait_ge(s_sp[1].h, p1_spill_n[1])
                    sy.wait_ge(s_spb.h, s_spb.n)
                s_in2[par].inc(sy.dma_start(out=kc[par][:], in_=s_k[c]), 16)
                s_in2[par].inc(sy.dma_start(out=qc[par][:], in_=s_q[c]), 16)
                s_in2[par].inc(sy.dma_start(out=vc[par][:], in_=s_v[c]), 16)
                if c >= 2:
                    a_.wait_ge(s_te.h, te_end[c - 2])
                    a_.wait_ge(s_dv.h, dv_end[c - 2])
                s_in2[par].inc(a_.dma_start(out=gc[par][:], in_=s_g[c]), 16)
                s_in2[par].inc(a_.dma_start(out=ac2[par][:], in_=s_a[c]), 16)
                s_in2[par].inc(a_.dma_start(
                    out=bbc[par][:],
                    in_=s_beta[c].unsqueeze(0).broadcast_to([128, H * C])), 16)
                in_ld = s_in2[par].n

                v_.wait_ge(s_in2[par].h, in_ld)
                v_.reciprocal(out=ra[:], in_=ac2[par][:])
                v_.drain()
                for tr in range(H):
                    v_.tensor_tensor_scan(
                        out=Pt[:, tr * (C + 1) + 1:(tr + 1) * (C + 1)],
                        data0=ac2[par][:, tr * C:(tr + 1) * C],
                        data1=zz[:], initial=onesf[:, 0:1],
                        op0=OP.mult, op1=OP.add)
                    v_.drain()
                v_.reciprocal(out=rp[:], in_=Pt[:])
                v_.drain()
                rp3 = rp[:].rearrange("p (h c) -> p h c", c=C + 1)
                Pt3 = Pt[:].rearrange("p (h c) -> p h c", c=C + 1)
                v_.tensor_copy(out=rPm[:].unsqueeze(2),
                               in_=rp3[:, :, MID + 1:MID + 2])
                v_.drain()
                v_.tensor_tensor(out=RC[:].unsqueeze(2), in0=Pt3[:, :, C:C + 1],
                                 in1=rPm[:].unsqueeze(2), op=OP.mult)
                v_.drain()
                for tr in range(H):
                    v_.scalar_tensor_tensor(
                        out=cv[:, tr * C:(tr + 1) * C],
                        in0=rp[:, tr * (C + 1):tr * (C + 1) + C],
                        scalar=Pt[:, tr * (C + 1) + MID + 1:tr * (C + 1) + MID + 2],
                        in1=vc[par][:, tr * C:(tr + 1) * C],
                        op0=OP.mult, op1=OP.mult)
                    v_.tensor_scalar(
                        out=Rt[:, tr * C:(tr + 1) * C],
                        in0=Pt[:, tr * (C + 1) + 1:(tr + 1) * (C + 1)],
                        scalar1=rPm[:, tr:tr + 1], scalar2=None, op0=OP.mult)
                    v_.drain()
                v_.wait_ge(s_in2[par].h, in_ld)
                v_.tensor_tensor(out=kpc[:], in0=kc[par][:], in1=bbc[par][:], op=OP.mult)
                v_.drain()
                v_.wait_ge(s_dv.h, dv_state[c - 1])
                for tr in range(H):
                    ts_op = v_.tensor_scalar(
                        out=tmpSc[:, tr * 128:(tr + 1) * 128],
                        in0=Sprev[:, tr * 128:(tr + 1) * 128],
                        scalar1=Pt[:, tr * (C + 1) + MID + 1:tr * (C + 1) + MID + 2],
                        scalar2=None, op0=OP.mult)
                    if tr < H - 1:
                        v_.drain()
                s_dv.inc(ts_op)
                dv_pro = s_dv.n

                t_.wait_ge(s_dv.h, dv_pro)
                for tr in range(H):
                    slot = tr % 8
                    t_.wait_ge(s_ac.h, s_ac.n)
                    s_te.inc(t_.transpose(out=psTb[:, slot * 128:(slot + 1) * 128],
                                          in_=tmpSc[:, tr * 128:(tr + 1) * 128],
                                          identity=idt[:]))
                    a_.wait_ge(s_te.h, s_te.n)
                    s_ac.inc(a_.copy(out=S0mT[:, tr * 128:(tr + 1) * 128],
                                     in_=psTb[:, slot * 128:(slot + 1) * 128]))
                ac_s0mt = s_ac.n

                for tr in range(H):
                    t_.wait_ge(s_dv.h, dv_pro)
                    mmg = t_.matmul(out=psG[:, tr * C:(tr + 1) * C],
                                    lhsT=kpc[:, tr * C:(tr + 1) * C],
                                    rhs=kc[par][:, tr * C:(tr + 1) * C],
                                    start=(tr == 0), stop=(tr == H - 1))
                    mmp = t_.matmul(out=psPq[:, tr * C:(tr + 1) * C],
                                    lhsT=kpc[:, tr * C:(tr + 1) * C],
                                    rhs=qc[par][:, tr * C:(tr + 1) * C],
                                    start=(tr == 0), stop=(tr == H - 1))
                s_te.inc(mmp)
                te_gram = s_te.n
                a_.wait_ge(s_te.h, te_gram)
                s_ac.inc(a_.copy(out=Gst[:], in_=psG[:]))
                ac_gs = s_ac.n
                v_.wait_ge(s_te.h, te_gram)
                mask3 = maskt[:].unsqueeze(1).broadcast_to([C, H, C])
                s_dv.inc(v_.tensor_tensor(
                    out=Pqm[:].rearrange("p (h c) -> p h c", c=C),
                    in0=psPq[:].rearrange("p (h c) -> p h c", c=C),
                    in1=mask3, op=OP.mult))
                dv_pqm = s_dv.n

                g.wait_ge(s_ac.h, ac_gs)
                if c >= 2:
                    g.wait_ge(s_te.h, te_end[c - 2])
                for s in range(NSUB):
                    s_gst[par].inc(g.dma_start(
                        out=s_Gar[par, s].rearrange("(j x) -> j x", x=H * SUB),
                        in_=Gst[SUB * s:SUB * s + SUB, :]
                            .rearrange("p (h c) -> p h c", c=C)[:, :, SUB * s:SUB * s + SUB]),
                        16)
                if c < 2:
                    g.wait_ge(s_gz.h, s_gz.n)
                for s in range(NSUB - 1):
                    s_gbw[par].inc(g.dma_start(
                        out=gbd_write_ap(par, s),
                        in_=Gst[SUB * s:SUB * s + SUB, :]
                            .rearrange("p (h c) -> p h c", c=C)[:, :, SUB * (s + 1):]),
                        16)
                gst_n = s_gst[par].n
                gbw_n = s_gbw[par].n

                t_.wait_ge(s_ac.h, ac_s0mt)
                t_.wait_ge(s_ac.h, ac_gs)
                t_.wait_ge(s_dv.h, dv_pqm)
                if c >= 1:
                    t_.wait_ge(s_dv.h, dv_end[c - 1])
                for tr in range(H):
                    mb_ = t_.matmul(out=psB[:, tr * C:(tr + 1) * C],
                                    lhsT=S0mT[:, tr * 128:(tr + 1) * 128],
                                    rhs=kc[par][:, tr * C:(tr + 1) * C],
                                    start=(tr == 0), stop=False,
                                    skip_group_check=True)
                    mo_ = t_.matmul(out=psO[:, tr * C:(tr + 1) * C],
                                    lhsT=S0mT[:, tr * 128:(tr + 1) * 128],
                                    rhs=qc[par][:, tr * C:(tr + 1) * C],
                                    start=(tr == 0), stop=False)
                s_te.inc(mo_)
                te_base = s_te.n

                sub_dv = {}
                roll_te = {-1: te_base}
                gbr_cnt = {}

                sy.wait_ge(s_gst[par].h, gst_n)
                if c >= 2:
                    sy.wait_ge(s_dv.h, dv_end[c - 2])
                s_gb2[par].inc(sy.dma_start(
                    out=Gb[par][:],
                    in_=s_Gar[par].rearrange("s x -> (s x)").unsqueeze(0)
                        .broadcast_to([128, NSUB * SUB * H * SUB])), 16)
                sy.wait_ge(s_gbw[par].h, gbw_n)
                if c >= 2:
                    sy.wait_ge(s_te.h, te_end[c - 2])
                s_gbr[par].inc(sy.dma_start(
                    out=Gbd[par][:], in_=s_gbd[par]), 16)
                gbr_all = s_gbr[par].n
                for s in range(NSUB):
                    sl = par
                    v_.wait_ge(s_te.h, roll_te[s - 1])
                    cv3 = cv[:].rearrange("p (h c) -> p h c", c=C)
                    U3 = U[:].rearrange("p (h c) -> p h c", c=C)
                    psB3 = psB.rearrange("p (h c) -> p h c", c=C)
                    v_.tensor_tensor(out=U3[:, :, SUB * s:SUB * (s + 1)],
                                     in0=cv3[:, :, SUB * s:SUB * (s + 1)],
                                     in1=psB3[:, :, SUB * s:SUB * (s + 1)],
                                     op=OP.subtract)
                    v_.drain()
                    v_.wait_ge(s_gb2[sl].h, s_gb2[sl].n)
                    ra3 = ra[:].rearrange("p (h c) -> p h c", c=C)
                    Y3 = Y[:].rearrange("p (h c) -> p h c", c=C)
                    Gb3 = Gb[sl][:, s * SUB * H * SUB:(s + 1) * SUB * H * SUB]\
                        .rearrange("p (j h t) -> p j h t", j=SUB, t=SUB)
                    for j in range(SUB):
                        t0 = SUB * s + j
                        last_op = v_.tensor_tensor(out=Y3[:, :, t0:t0 + 1],
                                                   in0=ra3[:, :, t0:t0 + 1],
                                                   in1=U3[:, :, t0:t0 + 1], op=OP.mult)
                        v_.drain()
                        if j < SUB - 1:
                            nrem = SUB - 1 - j
                            v_.tensor_tensor(
                                out=prod[:, 0:H * nrem].rearrange("p (h t) -> p h t", t=nrem),
                                in0=Y3[:, :, t0:t0 + 1].broadcast_to([128, H, nrem]),
                                in1=Gb3[:, j, :, j + 1:SUB], op=OP.mult)
                            v_.drain()
                            last_op = v_.tensor_tensor(
                                out=U3[:, :, t0 + 1:SUB * (s + 1)],
                                in0=U3[:, :, t0 + 1:SUB * (s + 1)],
                                in1=prod[:, 0:H * nrem].rearrange("p (h t) -> p h t", t=nrem),
                                op=OP.subtract)
                            v_.drain()
                    s_dv.inc(last_op)
                    sub_dv[s] = s_dv.n

                    if s < NSUB - 1:
                        s_dv.inc(v_.tensor_copy(
                            out=Ybnd[:].rearrange("p (h t) -> p h t", t=SUB),
                            in_=Y3[:, :, SUB * s:SUB * (s + 1)]))
                        v_.drain()
                        dv_ybnd = s_dv.n
                        slot = s % 8
                        t_.wait_ge(s_dv.h, dv_ybnd)
                        t_.wait_ge(s_ac.h, s_ac.n)
                        s_te.inc(t_.transpose(out=psTb[:, slot * 128:(slot + 1) * 128],
                                              in_=Ybnd[:], identity=idt[:]))
                        a_.wait_ge(s_te.h, s_te.n)
                        s_ac.inc(a_.copy(out=YTb[:], in_=psTb[:, slot * 128:(slot + 1) * 128]))
                        ac_ytb = s_ac.n
                        W = Ws[s]
                        t_.wait_ge(s_ac.h, ac_ytb)
                        t_.wait_ge(s_gbr[sl].h, gbr_all)
                        t_.wait_ge(s_dv.h, sub_dv[s])
                        for tr in range(H):
                            mr = t_.matmul(
                                out=psB[:, tr * C + SUB * (s + 1):(tr + 1) * C],
                                lhsT=YTb[:],
                                rhs=Gbd[sl][:, gbd_off[s] + tr * W:
                                            gbd_off[s] + (tr + 1) * W],
                                start=False, stop=(s == NSUB - 2 and tr == H - 1),
                                skip_group_check=True)
                        s_te.inc(mr)
                        roll_te[s] = s_te.n
                    else:
                        roll_te[s] = roll_te[s - 1]

                v_.wait_ge(s_dv.h, sub_dv[NSUB - 1])
                s_dv.inc(v_.tensor_copy(out=Ybf[:], in_=Y[:]))
                dv_ybf = s_dv.n
                for tr in range(H):
                    slot = tr % 4
                    t_.wait_ge(s_dv.h, dv_ybf)
                    t_.wait_ge(s_ac.h, s_ac.n)
                    s_te.inc(t_.transpose(out=psTb[0:C, slot * 128:(slot + 1) * 128],
                                          in_=Ybf[:, tr * C:(tr + 1) * C],
                                          identity=idt[:]))
                    a_.wait_ge(s_te.h, s_te.n)
                    s_ac.inc(a_.copy(out=ZTt[:, tr * 128:(tr + 1) * 128],
                                     in_=psTb[0:C, slot * 128:(slot + 1) * 128]))
                    slot2 = 4 + tr % 4
                    t_.wait_ge(s_ac.h, s_ac.n)
                    s_te.inc(t_.transpose(out=psTb[0:C, slot2 * 128:(slot2 + 1) * 128],
                                          in_=kpc[:, tr * C:(tr + 1) * C],
                                          identity=idt[:]))
                    a_.wait_ge(s_te.h, s_te.n)
                    s_ac.inc(a_.copy(out=Ktm[:, tr * 128:(tr + 1) * 128],
                                     in_=psTb[0:C, slot2 * 128:(slot2 + 1) * 128]))
                ac_zt = s_ac.n

                t_.wait_ge(s_ac.h, ac_zt)
                t_.wait_ge(s_dv.h, dv_pqm)
                for tr in range(H):
                    mo2 = t_.matmul(out=psO[:, tr * C:(tr + 1) * C],
                                    lhsT=ZTt[:, tr * 128:(tr + 1) * 128],
                                    rhs=Pqm[:, tr * C:(tr + 1) * C],
                                    start=False, stop=(tr == H - 1))
                s_te.inc(mo2)
                te_o = s_te.n

                st_dv_prev = {}
                for tr in range(H):
                    if tr >= 2:
                        t_.wait_ge(s_dv.h, st_dv_prev[tr - 2])
                    ms = t_.matmul(out=psSt(tr), lhsT=ZTt[:, tr * 128:(tr + 1) * 128],
                                   rhs=Ktm[:, tr * 128:(tr + 1) * 128],
                                   start=True, stop=True,
                                   skip_group_check=True)
                    s_te.inc(ms)
                    v_.wait_ge(s_te.h, s_te.n)
                    v_.tensor_scalar(out=rcb[tr % 2][:], in0=psSt(tr),
                                     scalar1=RC[:, tr:tr + 1], scalar2=None, op0=OP.mult)
                    v_.drain()
                    s_dv.inc(v_.scalar_tensor_tensor(
                        out=Snew[:, tr * 128:(tr + 1) * 128],
                        in0=Sprev[:, tr * 128:(tr + 1) * 128],
                        scalar=Pt[:, tr * (C + 1) + C:tr * (C + 1) + C + 1],
                        in1=rcb[tr % 2][:], op0=OP.mult, op1=OP.add))
                    st_dv_prev[tr] = s_dv.n
                dv_state[c] = s_dv.n

                v_.wait_ge(s_te.h, te_o)
                s_dv.inc(v_.tensor_tensor(out=osc[:], in0=psO[:], in1=Rt[:], op=OP.mult))
                dv_osc = s_dv.n
                v_.drain()
                s_dv.inc(v_.tensor_copy(out=obf2[:], in_=osc[:]))
                dv_obf = s_dv.n
                a_.wait_ge(s_dv.h, dv_osc)
                s_ac.inc(a_.activation(out=sqb[:], in_=osc[:], func=AF.Square))
                ac_sq = s_ac.n
                t_.wait_ge(s_dv.h, dv_obf)
                t_.wait_ge(s_ac.h, ac_sq)
                for rnd, rhs_t in ((0, obf2), (1, sqb)):
                    NHLF = H * C // 512
                    for ii in range(NHLF):
                        mst = t_.matmul(out=psStats[:, ii * 512:(ii + 1) * 512],
                                        lhsT=ones_sb[:],
                                        rhs=rhs_t[:, ii * 512:(ii + 1) * 512],
                                        start=True, stop=True)
                    s_te.inc(mst)
                    a_.wait_ge(s_te.h, s_te.n)
                    s_ac.inc(a_.copy(out=stbuf[0:1, rnd * H * C:(rnd + 1) * H * C],
                                     in_=psStats[:, 0:H * C]))
                    if rnd == 0:
                        t_.wait_ge(s_ac.h, s_ac.n)
                ac_st = s_ac.n
                g.wait_ge(s_ac.h, ac_st)
                if c >= 2:
                    g.wait_ge(s_dv.h, dv_end[c - 2])
                s_sst[par].inc(g.dma_start(out=s_stats[par, :].unsqueeze(0), in_=stbuf[:]), 16)
                g.wait_ge(s_sst[par].h, s_sst[par].n)
                s_stb[par].inc(g.dma_start(
                    out=stb[:], in_=s_stats[par, :].unsqueeze(0)
                        .broadcast_to([128, 2 * H * C])), 16)
                v_.wait_ge(s_stb[par].h, s_stb[par].n)
                s_dv.inc(v_.tensor_scalar(out=mu[:], in0=stb[:, 0:H * C],
                                          scalar1=1.0 / 128, scalar2=None, op0=OP.mult))
                a_.wait_ge(s_dv.h, s_dv.n)
                s_ac.inc(a_.activation(out=varb[:], in_=mu[:], func=AF.Square))
                v_.wait_ge(s_ac.h, s_ac.n)
                s_dv.inc(v_.scalar_tensor_tensor(out=varb[:], in0=stb[:, H * C:2 * H * C],
                                                 scalar=1.0 / 128, in1=varb[:],
                                                 op0=OP.mult, op1=OP.subtract))
                a_.wait_ge(s_dv.h, s_dv.n)
                s_ac.inc(a_.activation(out=varb[:], in_=varb[:], func=AF.Sqrt,
                                       bias=epst[:, 0:1]))
                v_.wait_ge(s_ac.h, s_ac.n)
                v_.reciprocal(out=varb[:], in_=varb[:])
                v_.drain()
                v_.tensor_tensor(out=t1b[:], in0=osc[:], in1=mu[:], op=OP.subtract)
                v_.drain()
                v_.tensor_tensor(out=t1b[:], in0=t1b[:], in1=varb[:], op=OP.mult)
                v_.drain()
                v_.scalar_tensor_tensor(out=t1b[:], in0=t1b[:], scalar=lngt[:, 0:1],
                                        in1=lnbt[:, 0:1].broadcast_to([128, H * C]),
                                        op0=OP.mult, op1=OP.add)
                v_.drain()
                og3 = og[:].rearrange("p (h t) -> p h t", t=T)
                s_dv.inc(v_.tensor_tensor(
                    out=og3[:, :, c * C:(c + 1) * C],
                    in0=t1b[:].rearrange("p (h c) -> p h c", c=C),
                    in1=gc[par][:].rearrange("p (h c) -> p h c", c=C), op=OP.mult))

                te_end[c] = s_te.n
                dv_end[c] = s_dv.n
                ac_end[c] = s_ac.n

            p2_te_end = s_te.n
            p2_dv_end = s_dv.n
            p2_ac_end = s_ac.n

            if "s_ogdt" in dump:
                g.wait_ge(s_dv.h, p2_dv_end)
                s_ogd.inc(g.dma_start(out=s_ogdt[:], in_=og[:]), 16)
                g.wait_ge(s_ogd.h, s_ogd.n)


        if cfg.get("p2_only", False):
            og_cm.__exit__(None, None, None)
            zt = E(nc.sbuf_tensor("zt_fin", [128, D], F32))
            nc.vector.wait_ge(s_te.h, s_te.n)
            nc.vector.wait_ge(s_ac.h, s_ac.n)
            nc.vector.wait_ge(s_ogd.h, s_ogd.n)
            nc.vector.drain()
            s_dv.inc(nc.vector.memset(zt[:], 0.0))
            g.wait_ge(s_dv.h, s_dv.n)
            for r in range(ceil_div(HT, 128)):
                nr = min(128, HT - r * 128)
                s_out.inc(g.dma_start(out=y_out[r * 128:r * 128 + nr, :],
                                      in_=zt[0:nr, :]), 16)
            g.wait_ge(s_out.h, s_out.n)
            return nc

        NTOK = HT // 128
        ps3 = E(nc.psum_tensor("ps3", [128, 4096], F32))
        with ExitStack() as p3ctx:
            E3 = p3ctx.enter_context
            woT = E3(nc.sbuf_tensor("woT", [128, H * D], BF16))
            xrt = [E3(nc.sbuf_tensor(f"xrt{i}", [128, D], F32)) for i in range(2)]
            t2f = E3(nc.sbuf_tensor("t2f", [128, D], F32))
            sqt = E3(nc.sbuf_tensor("sqt", [128, D], F32))
            x2ft = [E3(nc.sbuf_tensor(f"x2ft{i}", [128, D], F32)) for i in range(2)]
            x2bt = [E3(nc.sbuf_tensor(f"x2bt{i}", [128, D], BF16)) for i in range(2)]
            colt = E3(nc.sbuf_tensor("colt", [128, 8], F32))
            eps3 = E3(nc.sbuf_tensor("eps3", [128, 1], F32))
            s_woT = Sem(nc, E, "s_woT")
            s_xr = [Sem(nc, E, f"s_xr{i}") for i in range(2)]
            s_x2s = [Sem(nc, E, f"s_x2s{i}") for i in range(2)]

            for eng in (v_, a_, t_, sy, g):
                eng.wait_ge(s_te.h, p2_te_end)
                eng.wait_ge(s_dv.h, p2_dv_end)
                eng.wait_ge(s_ac.h, p2_ac_end)
            s_dv.inc(v_.memset(eps3[:], 1e-5))

            sy.wait_ge(s_p0w["o"].h, s_p0w["o"].n)
            for i2 in range(2):
                s_woT.inc(sy.dma_start(
                    out=woT[:].rearrange("p (dt f) -> p dt f", f=D)[:, i2 * 8:(i2 + 1) * 8, :],
                    in_=s_wbf["o"][:].rearrange("(dt p) f -> p dt f", p=128)
                        [:, i2 * 8:(i2 + 1) * 8, :]), 16)

            engs = [nc.tensor, nc.vector, nc.scalar, nc.sync, nc.gpsimd]
            for eng in engs:
                eng.wait_ge(s_con.h, s_con.n)
            rv = nc.values_load(flagt[0:1, 0:1],
                                engines=[mybir.EngineType.PE, mybir.EngineType.DVE,
                                         mybir.EngineType.Activation, mybir.EngineType.SP,
                                         mybir.EngineType.Pool])
            _sw_sems = [s_xr[0], s_xr[1], s_x2s[0], s_x2s[1], s_te, s_dv, s_ac]
            _sw_snap = [s.n for s in _sw_sems]
            for half in nc.Switch(engines=engs, index=[rv] * 5, n=2):
                if half == 1:
                    for _s, _v in zip(_sw_sems, _sw_snap):
                        _s.n = _v
                h0 = half * HT
                p3_dvj = {}
                for j in range(NTOK):
                    par3 = j % 2
                    tok = h0 + j * 128
                    if j >= 2:
                        sy.wait_ge(s_dv.h, s_dv.n)
                    s_xr[par3].inc(sy.dma_start(out=xrt[par3][:],
                                                in_=x_seq[tok:tok + 128, :]), 16)
                    t_.wait_ge(s_woT.h, s_woT.n)
                    pwo = (j % 2) * 2048
                    if j >= 2:
                        t_.wait_ge(s_dv.h, p3_dvj[j - 2])
                    for dcol in range(4):
                        for ft in range(H):
                            mw = t_.matmul(
                                out=ps3[:, pwo + dcol * 512:pwo + (dcol + 1) * 512],
                                lhsT=og[:, ft * T + h0 + j * 128:
                                        ft * T + h0 + (j + 1) * 128],
                                rhs=woT[:, ft * D + dcol * 512:ft * D + (dcol + 1) * 512],
                                start=(ft == 0), stop=(ft == H - 1))
                        s_te.inc(mw)
                    te_w = s_te.n
                    v_.wait_ge(s_te.h, te_w)
                    v_.wait_ge(s_xr[par3].h, s_xr[par3].n)
                    v_.tensor_tensor(out=t2f[:], in0=ps3[:, pwo:pwo + 2048],
                                     in1=xrt[par3][:], op=OP.add)
                    v_.drain()
                    s_dv.inc(v_.tensor_reduce(out=colt[:, 0:1].unsqueeze(2),
                                              in_=t2f[:].unsqueeze(1),
                                              axis=AX, op=OP.add))
                    a_.wait_ge(s_dv.h, s_dv.n)
                    s_ac.inc(a_.activation(out=sqt[:], in_=t2f[:], func=AF.Square,
                                           accum_out=colt[:, 1:2]))
                    v_.wait_ge(s_ac.h, s_ac.n)
                    v_.tensor_scalar(out=colt[:, 2:3], in0=colt[:, 0:1],
                                     scalar1=1.0 / D, scalar2=None, op0=OP.mult)
                    v_.drain()
                    v_.tensor_tensor(out=colt[:, 3:4], in0=colt[:, 2:3],
                                     in1=colt[:, 2:3], op=OP.mult)
                    v_.drain()
                    s_dv.inc(v_.scalar_tensor_tensor(out=colt[:, 4:5], in0=colt[:, 1:2],
                                                     scalar=1.0 / D, in1=colt[:, 3:4],
                                                     op0=OP.mult, op1=OP.subtract))
                    a_.wait_ge(s_dv.h, s_dv.n)
                    s_ac.inc(a_.activation(out=colt[:, 5:6], in_=colt[:, 4:5],
                                           func=AF.Sqrt, bias=eps3[:, 0:1]))
                    v_.wait_ge(s_ac.h, s_ac.n)
                    v_.reciprocal(out=colt[:, 6:7], in_=colt[:, 5:6])
                    v_.drain()
                    v_.tensor_scalar(out=x2ft[par3][:], in0=t2f[:],
                                     scalar1=colt[:, 2:3], scalar2=colt[:, 6:7],
                                     op0=OP.subtract, op1=OP.mult)
                    v_.drain()
                    s_dv.inc(v_.tensor_copy(out=x2bt[par3][:], in_=x2ft[par3][:]))
                    p3_dvj[j] = s_dv.n
                    g.wait_ge(s_dv.h, s_dv.n)
                    s_x2s[par3].inc(g.dma_start(out=s_x2f[j * 128:(j + 1) * 128, :],
                                                in_=x2ft[par3][:]), 16)
                    s_x2s[par3].inc(g.dma_start(out=s_x2bf[j * 128:(j + 1) * 128, :],
                                                in_=x2bt[par3][:]), 16)
            x2s_n = [s_x2s[0].n, s_x2s[1].n]
            p3_te = s_te.n
            p3_dv = s_dv.n
            p3_ac = s_ac.n

        og_cm.__exit__(None, None, None)
        with ExitStack() as p4ctx:
            E4 = p4ctx.enter_context
            wrT = E4(nc.sbuf_tensor("wrT", [128, H * NE], BF16))
            brb = E4(nc.sbuf_tensor("brb", [128, NE], F32))
            iot = E4(nc.sbuf_tensor("iot", [128, NE], F32))
            lgt = E4(nc.sbuf_tensor("lgt", [128, NE], F32))
            mxt = E4(nc.sbuf_tensor("mxt", [128, NE], F32))
            ixt = E4(nc.sbuf_tensor("ixt", [128, NE], U32))
            ixf = E4(nc.sbuf_tensor("ixf", [128, 2], F32))
            p12 = E4(nc.sbuf_tensor("p12", [128, 4], F32))
            scall = E4(nc.sbuf_tensor("scall", [128, NE * NTOK], F32))
            scp = E4(nc.sbuf_tensor("scp", [128, 32], F32))
            scpb = E4(nc.sbuf_tensor("scpb", [128, 32], BF16))
            scTb = E4(nc.sbuf_tensor("scTb", [32, HT], BF16))
            gubt = E4(nc.sbuf_tensor("gubt", [1, NE * 2 * DI], BF16))
            onesrow = E4(nc.sbuf_tensor("onesrow", [1, 128], BF16))
            dwbt = E4(nc.sbuf_tensor("dwbt", [NE, D], BF16))

            s_x2T = Sem(nc, E, "s_x2T")
            s_wrT2 = Sem(nc, E, "s_wrT2")
            s_gup2 = [Sem(nc, E, f"s_gup2{i}") for i in range(2)]
            s_acts = [Sem(nc, E, f"s_acts{i}") for i in range(2)]
            s_brb = Sem(nc, E, "s_brb")

            for eng in (v_, a_, t_, sy, g):
                eng.wait_ge(s_te.h, p3_te)
                eng.wait_ge(s_dv.h, p3_dv)
                eng.wait_ge(s_ac.h, p3_ac)
            guctx = ExitStack()
            E4g = guctx.enter_context
            x2T = E4g(nc.sbuf_tensor("x2T", [128, H * HT], BF16))
            sy.wait_ge(s_x2s[0].h, x2s_n[0])
            sy.wait_ge(s_x2s[1].h, x2s_n[1])
            TG = min(512, HT)
            for dt_i in range(16):
                for tg in range(HT // TG):
                    s_x2T.inc(sy.dma_start_transpose(
                        out=x2T[:, dt_i * HT + tg * TG:dt_i * HT + (tg + 1) * TG],
                        in_=s_x2bf[tg * TG:(tg + 1) * TG,
                                   dt_i * 128:(dt_i + 1) * 128]), 16)
            sy.wait_ge(s_p0s.h, s_p0s.n)
            s_wrT2.inc(sy.dma_start(
                out=wrT[:].rearrange("p (dt e) -> p dt e", e=NE),
                in_=s_wbf_r[:].rearrange("(dt p) e -> p dt e", p=128)), 16)
            s_brb.inc(sy.dma_start(out=brb[:], in_=br[:].unsqueeze(0)
                                   .broadcast_to([128, NE])), 16)
            s_brb.inc(sy.dma_start(out=iot[:], in_=iota8[:].unsqueeze(0)
                                   .broadcast_to([128, NE])), 16)
            sy.wait_ge(s_p0f.h, s_p0f.n)
            s_brb.inc(sy.dma_start(out=gubt[:],
                                   in_=s_gubbf[:].rearrange("e j -> (e j)").unsqueeze(0)), 16)
            s_dv.inc(v_.memset(onesrow[:], 1.0))
            s_brb.inc(sy.dma_start(out=dwbt[:], in_=s_dwbbf[:]), 16)

            v_.memset(scp[:], 0.0)
            v_.drain()
            for j in range(NTOK):
                t_.wait_ge(s_x2T.h, s_x2T.n)
                t_.wait_ge(s_wrT2.h, s_wrT2.n)
                if j >= 1:
                    t_.wait_ge(s_dv.h, s_dv.n)
                    t_.wait_ge(s_ac.h, s_ac.n)
                for dt_i in range(16):
                    mr4 = t_.matmul(out=ps3[:, 3584:3584 + NE],
                                    lhsT=x2T[:, dt_i * HT + j * 128:dt_i * HT + (j + 1) * 128],
                                    rhs=wrT[:, dt_i * NE:(dt_i + 1) * NE],
                                    start=(dt_i == 0), stop=(dt_i == 15))
                s_te.inc(mr4)
                v_.wait_ge(s_te.h, s_te.n)
                v_.wait_ge(s_brb.h, s_brb.n)
                v_.tensor_tensor(out=lgt[:], in0=ps3[:, 3584:3584 + NE], in1=brb[:],
                                 op=OP.add)
                v_.drain()
                v_.max(out=mxt[:], in_=lgt[:])
                v_.drain()
                v_.max_index(out=ixt[:], in_max=mxt[:], in_values=lgt[:])
                v_.drain()
                v_.tensor_copy(out=ixf[:], in_=ixt[:, 0:2])
                v_.drain()
                v_.tensor_tensor(out=p12[:, 2:3], in0=mxt[:, 0:1], in1=mxt[:, 1:2],
                                 op=OP.subtract)
                v_.drain()
                s_dv.inc(v_.tensor_copy(out=p12[:, 3:4], in_=p12[:, 2:3]))
                a_.wait_ge(s_dv.h, s_dv.n)
                s_ac.inc(a_.activation(out=p12[:, 0:1], in_=p12[:, 3:4], func=AF.Sigmoid))
                v_.wait_ge(s_ac.h, s_ac.n)
                v_.tensor_scalar(out=p12[:, 1:2], in0=p12[:, 0:1], scalar1=-1.0,
                                 scalar2=1.0, op0=OP.mult, op1=OP.add)
                v_.drain()
                sc_j = scall[:, j * NE:(j + 1) * NE]
                v_.tensor_scalar(out=sc_j, in0=iot[:], scalar1=ixf[:, 0:1],
                                 scalar2=None, op0=OP.is_equal)
                v_.drain()
                v_.tensor_scalar(out=sc_j, in0=sc_j, scalar1=p12[:, 0:1],
                                 scalar2=None, op0=OP.mult)
                v_.drain()
                v_.tensor_scalar(out=scp[:, 0:NE], in0=iot[:], scalar1=ixf[:, 1:2],
                                 scalar2=None, op0=OP.is_equal)
                v_.drain()
                v_.scalar_tensor_tensor(out=sc_j, in0=scp[:, 0:NE],
                                        scalar=p12[:, 1:2], in1=sc_j,
                                        op0=OP.mult, op1=OP.add)
                v_.drain()
                v_.memset(scp[:, 0:32], 0.0)
                v_.drain()
                v_.tensor_copy(out=scp[:, 0:NE], in_=sc_j)
                v_.drain()
                s_dv.inc(v_.tensor_copy(out=scpb[:], in_=scp[:, 0:32]))
                psT4s = ps3[:, 3968:4032].bitcast(BF16)
                t_.wait_ge(s_dv.h, s_dv.n)
                s_te.inc(t_.transpose(out=psT4s[0:32, 0:128], in_=scpb[:],
                                      identity=idt[:]))
                a_.wait_ge(s_te.h, s_te.n)
                s_ac.inc(a_.copy(out=scTb[0:32, j * 128:(j + 1) * 128],
                                 in_=psT4s[0:32, 0:128]))
            dv_scores = s_dv.n

            glu_dv = {}
            gupb = [E4g(nc.sbuf_tensor(f"gupb{i}", [128, 16 * 1024], BF16)) for i in range(2)]
            actrow = [E4g(nc.sbuf_tensor(f"actrow{i}", [128, DI], BF16)) for i in range(NTOK)]
            gmn = E4g(nc.sbuf_tensor("gmn", [128, 256], F32))
            upc = E4g(nc.sbuf_tensor("upc", [128, 256], F32))
            sgm = E4g(nc.sbuf_tensor("sgm", [128, 256], F32))
            actf = E4g(nc.sbuf_tensor("actf", [128, 256], F32))
            for e in range(NE):
                for jh in range(4):
                    sl4 = (e * 4 + jh) % 2
                    if e * 4 + jh >= 2:
                        a_.wait_ge(s_te.h, s_te.n)
                    s_gup2[sl4].inc(a_.dma_start(
                        out=gupb[sl4][:].rearrange("p (dt j) -> p dt j", j=1024),
                        in_=s_gupbf[e].rearrange("(dt p) j -> p dt j", p=128)
                            [:, :, jh * 1024:(jh + 1) * 1024]), 16)
                    for j512 in range(2):
                        jcol = jh * 1024 + j512 * 512
                        for j in range(NTOK):
                            bank = j % 8
                            t_.wait_ge(s_gup2[sl4].h, s_gup2[sl4].n)
                            t_.wait_ge(s_x2T.h, s_x2T.n)
                            if bank in glu_dv:
                                t_.wait_ge(s_dv.h, glu_dv[bank])
                            for dt_i in range(16):
                                mg = t_.matmul(
                                    out=ps3[:, bank * 512:(bank + 1) * 512],
                                    lhsT=x2T[:, dt_i * HT + j * 128:
                                             dt_i * HT + (j + 1) * 128],
                                    rhs=gupb[sl4][:, dt_i * 1024 + j512 * 512:
                                                  dt_i * 1024 + j512 * 512 + 512],
                                    start=(dt_i == 0), stop=False)
                            t_.wait_ge(s_dv.h, s_dv.n)
                            mg = t_.matmul(
                                out=ps3[:, bank * 512:(bank + 1) * 512],
                                lhsT=onesrow[:],
                                rhs=gubt[0:1, e * 2 * DI + jcol:e * 2 * DI + jcol + 512],
                                start=False, stop=True)
                            s_te.inc(mg)
                            v_.wait_ge(s_te.h, s_te.n)
                            pg = ps3[:, bank * 512:(bank + 1) * 512]
                            pg3 = pg.rearrange("p (a two) -> p a two", two=2)
                            v_.tensor_scalar(out=gmn[:], in0=pg3[:, :, 0:1].squeeze(2),
                                             scalar1=7.0, scalar2=None, op0=OP.min)
                            v_.drain()
                            s_dv.inc(v_.tensor_scalar(
                                out=upc[:], in0=pg3[:, :, 1:2].squeeze(2),
                                scalar1=7.0, scalar2=-7.0, op0=OP.min, op1=OP.max))
                            a_.wait_ge(s_dv.h, s_dv.n)
                            s_ac.inc(a_.activation(out=sgm[:], in_=gmn[:],
                                                   func=AF.Sigmoid, scale=1.702))
                            v_.wait_ge(s_ac.h, s_ac.n)
                            v_.tensor_tensor(out=sgm[:], in0=gmn[:], in1=sgm[:],
                                             op=OP.mult)
                            v_.drain()
                            v_.scalar_tensor_tensor(out=actf[:], in0=upc[:], scalar=1.0,
                                                    in1=sgm[:], op0=OP.add, op1=OP.mult)
                            v_.drain()
                            arow = actrow[j]
                            v_.wait_ge(s_acts[j % 2].h, s_acts[j % 2].n)
                            s_dv.inc(v_.tensor_scalar(
                                out=arow[:, jh * 512 + j512 * 256:
                                         jh * 512 + (j512 + 1) * 256],
                                in0=actf[:], scalar1=scall[:, j * NE + e:j * NE + e + 1],
                                scalar2=None, op0=OP.mult))
                            glu_dv[bank] = s_dv.n
                            v_.drain()
                            if jh == 3 and j512 == 1:
                                g.wait_ge(s_dv.h, s_dv.n)
                                s_acts[j % 2].inc(g.dma_start(
                                    out=s_act[e, j * 128:(j + 1) * 128, :],
                                    in_=arow[:]), 16)
            acts_n = [s_acts[0].n, s_acts[1].n]
            guctx.close()
            for eng in (v_, a_, t_, sy, g):
                eng.wait_ge(s_te.h, s_te.n)
                eng.wait_ge(s_dv.h, s_dv.n)
                eng.wait_ge(s_ac.h, s_ac.n)
                eng.wait_ge(s_acts[0].h, acts_n[0])
                eng.wait_ge(s_acts[1].h, acts_n[1])

            ffnout = E4(nc.sbuf_tensor("ffnout", [128, NTOK * 512 * 4 // 4], F32)) \
                if False else None
            p4f = E4(nc.sbuf_tensor("p4f", [128, NTOK * 2048], F32))
            dwnb = [E4(nc.sbuf_tensor(f"dwnb{i}", [128, 512], BF16)) for i in range(2)]
            actT4 = [E4(nc.sbuf_tensor(f"actT4{i}", [128, HT], BF16)) for i in range(2)]
            s_dwn2 = [Sem(nc, E, f"s_dwn2{i}") for i in range(2)]
            s_actT = [Sem(nc, E, f"s_actT{i}") for i in range(2)]

            for hcol in range(4):
                dwn_te = {}
                t_.wait_ge(s_dv.h, dv_scores)
                if hcol > 0:
                    t_.wait_ge(s_dv.h, s_dv.n)
                t_.wait_ge(s_brb.h, s_brb.n)
                for j in range(NTOK):
                    mb4 = t_.matmul(out=ps3[:, j * 512:(j + 1) * 512],
                                    lhsT=scTb[0:NE, j * 128:(j + 1) * 128],
                                    rhs=dwbt[0:NE, hcol * 512:(hcol + 1) * 512],
                                    start=True, stop=False, skip_group_check=True)
                s_te.inc(mb4)
                for e in range(NE):
                    for dt_i in range(16):
                        it = e * 16 + dt_i
                        slw = it % 2
                        if it >= 2:
                            g.wait_ge(s_te.h, dwn_te[it - 2])
                            sy.wait_ge(s_te.h, dwn_te[it - 2])
                        elif hcol > 0:
                            g.wait_ge(s_te.h, s_te.n)
                            sy.wait_ge(s_te.h, s_te.n)
                        g.wait_ge(s_dwn2[slw].h, s_dwn2[slw].n)
                        sy.wait_ge(s_actT[slw].h, s_actT[slw].n)
                        g.wait_ge(s_acts[0].h, acts_n[0]) if it == 0 else None
                        g.wait_ge(s_acts[1].h, acts_n[1]) if it == 0 else None
                        s_dwn2[slw].inc(g.dma_start(
                            out=dwnb[slw][:],
                            in_=s_dwnbf[e, dt_i * 128:(dt_i + 1) * 128,
                                        hcol * 512:(hcol + 1) * 512]), 16)
                        if it == 0:
                            sy.wait_ge(s_acts[0].h, acts_n[0])
                            sy.wait_ge(s_acts[1].h, acts_n[1])
                        s_actT[slw].inc(sy.dma_start_transpose(
                            out=actT4[slw][:],
                            in_=s_act[e, :, dt_i * 128:(dt_i + 1) * 128]), 16)
                        t_.wait_ge(s_dwn2[slw].h, s_dwn2[slw].n)
                        t_.wait_ge(s_actT[slw].h, s_actT[slw].n)
                        last = (e == NE - 1 and dt_i == 15)
                        for j in range(NTOK):
                            md4 = t_.matmul(
                                out=ps3[:, j * 512:(j + 1) * 512],
                                lhsT=actT4[slw][:, j * 128:(j + 1) * 128],
                                rhs=dwnb[slw][:],
                                start=False, stop=(last and j == NTOK - 1),
                                skip_group_check=True)
                        s_te.inc(md4)
                        dwn_te[it] = s_te.n
                v_.wait_ge(s_te.h, s_te.n)
                for j in range(NTOK):
                    ev = v_.tensor_copy(
                        out=p4f[:, j * 2048 + hcol * 512:j * 2048 + (hcol + 1) * 512],
                        in_=ps3[:, j * 512:(j + 1) * 512])
                s_dv.inc(ev)
                v_.drain()

            x2r4 = [E4(nc.sbuf_tensor(f"x2r4{i}", [128, D], F32)) for i in range(2)]
            yt4 = [E4(nc.sbuf_tensor(f"yt4{i}", [128, D], F32)) for i in range(2)]
            colt4 = E4(nc.sbuf_tensor("colt4", [128, 8], F32))
            sqt4 = E4(nc.sbuf_tensor("sqt4", [128, D], F32))
            eps4 = E4(nc.sbuf_tensor("eps4", [128, 1], F32))
            s_xr4 = [Sem(nc, E, f"s_xr4{i}") for i in range(2)]
            s_dv.inc(v_.memset(eps4[:], 1e-5))
            for j in range(NTOK):
                par4 = j % 2
                if j >= 2:
                    sy.wait_ge(s_dv.h, s_dv.n)
                sy.wait_ge(s_x2s[0].h, x2s_n[0])
                sy.wait_ge(s_x2s[1].h, x2s_n[1])
                s_xr4[par4].inc(sy.dma_start(out=x2r4[par4][:],
                                             in_=s_x2f[j * 128:(j + 1) * 128, :]), 16)
                v_.wait_ge(s_xr4[par4].h, s_xr4[par4].n)
                v_.tensor_tensor(out=yt4[par4][:], in0=p4f[:, j * 2048:(j + 1) * 2048],
                                 in1=x2r4[par4][:], op=OP.add)
                v_.drain()
                s_dv.inc(v_.tensor_reduce(out=colt4[:, 0:1].unsqueeze(2),
                                          in_=yt4[par4][:].unsqueeze(1),
                                          axis=AX, op=OP.add))
                a_.wait_ge(s_dv.h, s_dv.n)
                s_ac.inc(a_.activation(out=sqt4[:], in_=yt4[par4][:], func=AF.Square,
                                       accum_out=colt4[:, 1:2]))
                v_.wait_ge(s_ac.h, s_ac.n)
                v_.tensor_scalar(out=colt4[:, 2:3], in0=colt4[:, 0:1],
                                 scalar1=1.0 / D, scalar2=None, op0=OP.mult)
                v_.drain()
                v_.tensor_tensor(out=colt4[:, 3:4], in0=colt4[:, 2:3],
                                 in1=colt4[:, 2:3], op=OP.mult)
                v_.drain()
                s_dv.inc(v_.scalar_tensor_tensor(out=colt4[:, 4:5], in0=colt4[:, 1:2],
                                                 scalar=1.0 / D, in1=colt4[:, 3:4],
                                                 op0=OP.mult, op1=OP.subtract))
                a_.wait_ge(s_dv.h, s_dv.n)
                s_ac.inc(a_.activation(out=colt4[:, 5:6], in_=colt4[:, 4:5],
                                       func=AF.Sqrt, bias=eps4[:, 0:1]))
                v_.wait_ge(s_ac.h, s_ac.n)
                v_.reciprocal(out=colt4[:, 6:7], in_=colt4[:, 5:6])
                v_.drain()
                s_dv.inc(v_.tensor_scalar(out=yt4[par4][:], in0=yt4[par4][:],
                                          scalar1=colt4[:, 2:3], scalar2=colt4[:, 6:7],
                                          op0=OP.subtract, op1=OP.mult))
                sy.wait_ge(s_dv.h, s_dv.n)
                s_out.inc(sy.dma_start(out=y_out[j * 128:(j + 1) * 128, :],
                                       in_=yt4[par4][:]), 16)
            sy.wait_ge(s_out.h, s_out.n)

    return nc


    return nc



def _consts():
    return {
        "iden": np.eye(128, dtype=BF),
        "maskpq": np.triu(np.ones((C, C), np.float32)),
        "onescol": np.ones((128, 1), BF),
        "iota8": np.arange(8, dtype=np.float32),
    }


def _derived(p):
    d = {}
    for m in ("q", "k", "v", "a", "g", "o"):
        d[f"WT_{m}"] = np.ascontiguousarray(p["attn_W" + m].T)
    d["WbT_in"] = np.ascontiguousarray(p["attn_Wb"].T)
    d["WrT_in"] = np.ascontiguousarray(p["ffn_router_weight"].T)
    for m in ("q", "k", "v"):
        d[f"cwa_{m}"] = np.ascontiguousarray(
            p[f"attn_conv_{m}_w"].reshape(16, 128, 4).transpose(1, 0, 2).reshape(128, 64))
        d[f"cba_{m}"] = np.ascontiguousarray(
            p[f"attn_conv_{m}_b"].reshape(16, 128).T)
    d["ba_arr"] = np.ascontiguousarray(p["attn_ba"].reshape(16, 128).T)
    return d


def kernel(**inputs):
    cfg = dict(DEF_CFG)
    nc = build_program(cfg)
    p = {k[3:] if k.startswith("L0_") else k: v for k, v in inputs.items()}
    x = np.ascontiguousarray(p["attn_x"])
    B = x.shape[0]
    consts = _consts()
    shared = dict(
        Wq=p["attn_Wq"], Wk=p["attn_Wk"], Wv=p["attn_Wv"], Wa=p["attn_Wa"],
        Wg=p["attn_Wg"], Wo=p["attn_Wo"], ba=p["attn_ba"], Wb=p["attn_Wb"],
        bb=p["attn_bb"],
        conv_q_w=p["attn_conv_q_w"], conv_q_b=p["attn_conv_q_b"],
        conv_k_w=p["attn_conv_k_w"], conv_k_b=p["attn_conv_k_b"],
        conv_v_w=p["attn_conv_v_w"], conv_v_b=p["attn_conv_v_b"],
        ln_g=p["attn_ln_g"], ln_b=p["attn_ln_b"],
        router_w=p["ffn_router_weight"], router_b=p["ffn_router_bias"],
        gup=p["ffn_gate_up_proj"], gup_b=p["ffn_gate_up_proj_bias"],
        down=p["ffn_down_proj"], down_b=p["ffn_down_proj_bias"],
        **consts,
    )
    shared.update(_derived(p))
    in_maps = []
    xTs = [np.ascontiguousarray(x[b].T) for b in range(B)]
    for c in range(8):
        b, h = c // 2, c % 2
        m = dict(shared)
        m["x_seq"] = x[b]
        m["xT_in"] = xTs[b]
        m["half_flag"] = np.array([[h]], dtype=np.uint32)
        in_maps.append(m)
    res = run_bass_kernel_spmd(nc, in_maps, core_ids=list(range(8)))
    out = np.empty((B, 2048, 2048), np.float32)
    for c in range(8):
        b, h = c // 2, c % 2
        out[b, h * 1024:(h + 1) * 1024, :] = res.results[c]["y_out"]
    return out
